# version 19
# baseline (speedup 1.0000x reference)
import numpy as np
from contextlib import ExitStack

import concourse.bass as bass
import concourse.mybir as mybir
from concourse.bass_utils import run_bass_kernel_spmd

F32 = mybir.dt.float32
BF16 = mybir.dt.bfloat16
AF = mybir.ActivationFunctionType
ALU = mybir.AluOpType
AX = mybir.AxisListType

P = 128
TOK = 1024
TT = 8
D = 4096
NCH = 32
NH = 32
NKV = 8
S_FULL = 4096
KT = 32
LN_EPS = 1e-5
RMS_EPS = 1.1920929e-07
SCALE = float(1.0 / np.sqrt(128.0))
ESHIFT = -25.0
N_ITER = 50


class Sem:
    def __init__(self, h):
        self.h = h
        self.n = 0

    def inc(self, instr, k=1):
        instr.then_inc(self.h, k)
        self.n += k
        return self.n


def build_program(taps=()):
    nc = bass.Bass("TRN2", target_bir_lowering=False)

    x_in = nc.dram_tensor("x_in", [TOK, D], F32, kind="ExternalInput").ap()
    wqT = nc.dram_tensor("wqT", [D, D], F32, kind="ExternalInput").ap()
    wkT = nc.dram_tensor("wkT", [D, 1024], F32, kind="ExternalInput").ap()
    wvT = nc.dram_tensor("wvT", [D, 1024], F32, kind="ExternalInput").ap()
    woT = nc.dram_tensor("woT", [D, D], F32, kind="ExternalInput").ap()
    wbr = nc.dram_tensor("wbr", [1024, 1024], F32, kind="ExternalInput").ap()
    wlogT = nc.dram_tensor("wlogT", [D, 24], F32, kind="ExternalInput").ap()
    bases_in = nc.dram_tensor("bases_in", [P, 24], F32, kind="ExternalInput").ap()
    gate_in = nc.dram_tensor("gate_in", [P, 1], F32, kind="ExternalInput").ap()
    bv_in = nc.dram_tensor("bv_in", [P, NKV], F32, kind="ExternalInput").ap()

    out_d = nc.dram_tensor("out", [TOK, D], F32, kind="ExternalOutput").ap()
    x2_kw = dict(kind="ExternalOutput") if "x2" in taps else {}
    x2_d = nc.dram_tensor("x2", [TOK, D], F32, **x2_kw).ap()
    q_d = nc.dram_tensor("q_sc", [D, TOK], BF16).ap()
    kv_own = nc.dram_tensor("kv_own", [2048, TOK], BF16)
    kv_gath = nc.dram_tensor("kv_gath", [4 * 2048, TOK], BF16)
    tap_d = {}
    if "lnT" in taps:
        tap_d["tap_lnT"] = nc.dram_tensor("tap_lnT", [D, TOK], F32, kind="ExternalOutput").ap()
    if "kv" in taps:
        tap_d["tap_kv"] = nc.dram_tensor("tap_kv", [8192, TOK], F32, kind="ExternalOutput").ap()
    if "q" in taps:
        tap_d["tap_q"] = nc.dram_tensor("tap_q", [D, TOK], F32, kind="ExternalOutput").ap()
    if "ctxT" in taps:
        tap_d["tap_ctxT"] = nc.dram_tensor("tap_ctxT", [D, TOK], F32, kind="ExternalOutput").ap()
    if "ffn" in taps:
        tap_d["tap_ffn"] = nc.dram_tensor("tap_ffn", [P, 96], F32, kind="ExternalOutput").ap()
        tap_d["tap_y"] = nc.dram_tensor("tap_y", [TOK, 1024], F32, kind="ExternalOutput").ap()

    with ExitStack() as ctx:
        en = ctx.enter_context
        ident = en(nc.sbuf_tensor("ident", [P, P], BF16))
        wpan = en(nc.sbuf_tensor("wpan", [P, 2, NCH, P], BF16))
        lnT = en(nc.sbuf_tensor("lnT", [P, NCH, TOK], BF16))
        ctxT = en(nc.sbuf_tensor("ctxT", [P, NCH, TOK], BF16))
        pool_a = en(nc.sbuf_tensor("pool_a", [P, 8448], BF16))
        xt = en(nc.sbuf_tensor("xt", [P, D], F32))
        ln_tok = en(nc.sbuf_tensor("ln_tok", [P, D], BF16))
        qbuf = en(nc.sbuf_tensor("qbuf", [P, TOK], BF16))
        pbuf = en(nc.sbuf_tensor("pbuf", [P, 2, 1024], BF16))
        evac = en(nc.sbuf_tensor("evac", [P, 2, TOK], BF16))
        ctx_st = en(nc.sbuf_tensor("ctx_st", [P, 4, P], BF16))
        wlog_sb = en(nc.sbuf_tensor("wlog_sb", [P, NCH, 24], BF16))
        bases = en(nc.sbuf_tensor("bases", [P, 24], F32))
        gate_sb = en(nc.sbuf_tensor("gate_sb", [P, 1], F32))
        bv_sb = en(nc.sbuf_tensor("bv_sb", [P, NKV], F32))
        bnraw = en(nc.sbuf_tensor("bnraw", [P, 48], F32))
        mv = en(nc.sbuf_tensor("mv", [P, TT, 2], F32))
        rstd = en(nc.sbuf_tensor("rstd", [P, TT], F32))
        nmr = en(nc.sbuf_tensor("nmr", [P, TT], F32))
        rms = en(nc.sbuf_tensor("rms", [P, TT], F32))
        sc1 = en(nc.sbuf_tensor("sc1", [P, TT, 4], F32))
        hps = en(nc.sbuf_tensor("hps", [P, TT, 4], F32))
        rcp = en(nc.sbuf_tensor("rcp", [P, 4], F32))
        Z = en(nc.sbuf_tensor("Z", [P, TT, 16], F32))
        Ebuf = en(nc.sbuf_tensor("Ebuf", [P, TT, 16], F32))
        rsb = en(nc.sbuf_tensor("rsb", [P, TT, 4], F32))
        lsb = en(nc.sbuf_tensor("lsb", [P, TT, 4], F32))
        PL = en(nc.sbuf_tensor("PL", [P, TT, 8], F32))
        Hpre = en(nc.sbuf_tensor("Hpre", [P, TT, 4], F32))
        Hpost = en(nc.sbuf_tensor("Hpost", [P, TT, 4], F32))
        coef = en(nc.sbuf_tensor("coef", [P, TT, 16], F32))
        nb = en(nc.sbuf_tensor("nb", [P, TT, 4], F32))
        gtmp = en(nc.sbuf_tensor("gtmp", [P, 4], F32))
        hsc = en(nc.sbuf_tensor("hsc", [P, TT, 2], F32))
        cb = en(nc.sbuf_tensor("cb", [P, 4], F32))
        Zs = en(nc.sbuf_tensor("Zs", [P, TT, 16], F32))
        mxb = en(nc.sbuf_tensor("mxb", [P, TT, 4], F32))

        ps_s = en(nc.psum_tensor("ps_s", [P, 1024], F32))
        ps_ctx = en(nc.psum_tensor("ps_ctx", [P, 1024], F32))
        ps_tr = en(nc.psum_tensor("ps_tr", [P, 2048], BF16))
        ps_x = en(nc.psum_tensor("ps_x", [P, 1024], F32))

        sW = Sem(en(nc.semaphore("sW")))
        sW0 = Sem(en(nc.semaphore("sW0")))
        sW1 = Sem(en(nc.semaphore("sW1")))
        sX = Sem(en(nc.semaphore("sX")))
        sPE = Sem(en(nc.semaphore("sPE")))
        sV = Sem(en(nc.semaphore("sV")))
        sSC = Sem(en(nc.semaphore("sSC")))
        sG = Sem(en(nc.semaphore("sG")))
        sACC = Sem(en(nc.semaphore("sACC")))
        sPC = Sem(en(nc.semaphore("sPC")))
        sOUT = Sem(en(nc.semaphore("sOUT")))

        block = en(nc.Block())

        K_h = pool_a[:, 0:4096].rearrange("p (r t) -> p r t", r=4)
        V_sb = pool_a[:, 4096:8448].rearrange("p (c f) -> p c f", c=KT)
        v_stage = pool_a[:, 0:8192].rearrange("p (t f) -> p t f", t=TT)
        y_all = v_stage
        x_preT = ctxT[:, 0:8, :]
        xpb = ctxT[:, 12, :]
        xt2 = [xt[:], ctxT[:, 20:28, :].rearrange("p a b -> p (a b)").bitcast(F32)]
        ln2v = [ln_tok[:], ctxT[:, 28:32, :].rearrange("p a b -> p (a b)")]
        aoT = lnT[:, 14:20, :].rearrange("p a (j f) -> p a j f", j=TT)
        ostage = ctxT[:, 16:18, :].rearrange("p a b -> p (a b)").bitcast(F32)
        ostage2 = [ostage, ctxT[:, 18:20, :].rearrange("p a b -> p (a b)").bitcast(F32)]

        @block.sync
        def _(sy):
            sy.dma_start(out=bases[:], in_=bases_in[:, :]).then_inc(sX.h, 16)
            sy.dma_start(out=gate_sb[:], in_=gate_in[:, :]).then_inc(sX.h, 16)
            sy.dma_start(out=bv_sb[:], in_=bv_in[:, :]).then_inc(sX.h, 16)
            sX.n += 48
        tok_consts = sX.n

        @block.scalar
        def _(s):
            s.dma_start(out=x2_d[:, :], in_=x_in[:, :]).then_inc(sPC.h, 16)
            sPC.n += 16
        tok_x2pre = sPC.n

        @block.vector
        def _(v):
            v.wait_ge(sX.h, tok_consts)
            sV.inc(v.tensor_scalar(out=bases[:, 0:16], in0=bases[:, 0:16],
                                   scalar1=10.0, scalar2=None, op0=ALU.mult))
        tok_bases = sV.n

        @block.gpsimd
        def _(g):
            sG.inc(g.memset(ident[:], 0.0))
            g.wait_ge(sG.h, sG.n)
            sG.inc(g.affine_select(out=ident[:], in_=ident[:], compare_op=ALU.not_equal,
                                   fill=1.0, base=0, pattern=[[-1, P]], channel_multiplier=1))
            g.wait_ge(sG.h, sG.n)
            g.memset(cb[:, 0:1], LN_EPS)
            g.memset(cb[:, 1:2], RMS_EPS)
            sG.inc(g.memset(cb[:, 2:3], ESHIFT))
            g.wait_ge(sG.h, sG.n)
            g.dma_start(out=wlog_sb[:], in_=wlogT.rearrange("(c p) m -> p c m", p=P)).then_inc(sW.h, 16)
            sW.n += 16
        tok_ident = sG.n
        tok_wlog = sW.n

        ps_tr_free = [sV.n, sV.n]
        lnT_tile_tok = [0] * TT

        def emit_ln_tile(t, load_tok, want_rms, lnfree=None, xt_ap=None, ln_ap=None):
            xt_ap = xt[:] if xt_ap is None else xt_ap
            ln_ap = ln_tok[:] if ln_ap is None else ln_ap
            @block.vector
            def _(v):
                v.wait_ge(sX.h, load_tok)
                for k in range(8):
                    instr = v.bn_stats(bnraw[:, k * 6:(k + 1) * 6], xt_ap[:, k * 512:(k + 1) * 512])
                    sV.inc(instr)
                v.wait_ge(sV.h, sV.n)
                sV.inc(v.bn_aggr(mv[:, t], bnraw[:]))
            tok_aggr = sV.n

            @block.scalar
            def _(s):
                s.wait_ge(sV.h, tok_aggr)
                s.wait_ge(sG.h, tok_ident)
                sSC.inc(s.activation(out=sc1[:, t, 0:1], in_=mv[:, t, 1:2], func=AF.Ln,
                                     bias=cb[:, 0:1], scale=1.0))
                s.wait_ge(sSC.h, sSC.n)
                sSC.inc(s.activation(out=rstd[:, t:t + 1], in_=sc1[:, t, 0:1], func=AF.Exp,
                                     bias=0.0, scale=-0.5))
            tok_rstd = sSC.n

            @block.vector
            def _(v):
                v.wait_ge(sSC.h, tok_rstd)
                sV.inc(v.tensor_scalar(out=nmr[:, t:t + 1], in0=mv[:, t, 0:1],
                                       scalar1=-1.0, scalar2=None, op0=ALU.mult))
                if want_rms:
                    sV.inc(v.tensor_scalar(out=sc1[:, t, 1:2], in0=mv[:, t, 1:2], scalar1=LN_EPS,
                                           scalar2=None, op0=ALU.add))
                    v.wait_ge(sV.h, sV.n)
                    sV.inc(v.reciprocal(sc1[:, t, 2:3], sc1[:, t, 1:2]))
                    v.wait_ge(sV.h, sV.n)
                    sV.inc(v.tensor_tensor(out=sc1[:, t, 3:4], in0=mv[:, t, 1:2], in1=sc1[:, t, 2:3],
                                           op=ALU.mult))
            tok_nmr = sV.n

            if want_rms:
                @block.scalar
                def _(s):
                    s.wait_ge(sV.h, tok_nmr)
                    sSC.inc(s.activation(out=sc1[:, t, 0:1], in_=sc1[:, t, 3:4], func=AF.Ln,
                                         bias=cb[:, 1:2], scale=1.0))
                    s.wait_ge(sSC.h, sSC.n)
                    sSC.inc(s.activation(out=rms[:, t:t + 1], in_=sc1[:, t, 0:1], func=AF.Exp,
                                         bias=0.0, scale=-0.5))

            @block.vector
            def _(v):
                v.wait_ge(sV.h, tok_nmr)
                if lnfree is not None and lnfree[1]:
                    v.wait_ge(lnfree[0].h, lnfree[1])
                sV.inc(v.tensor_scalar(out=ln_ap, in0=xt_ap,
                                       scalar1=nmr[:, t:t + 1], scalar2=rstd[:, t:t + 1],
                                       op0=ALU.add, op1=ALU.mult))
            tok_lntok = sV.n

            @block.sync
            def _(sy):
                sy.wait_ge(sV.h, tok_lntok)
                for c in range(NCH):
                    sy.dma_start_transpose(
                        out=lnT[:, c, t * P:(t + 1) * P],
                        in_=ln_ap[:, c * P:(c + 1) * P]).then_inc(sX.h, 16)
                    sX.n += 16
            lnT_tile_tok[t] = sX.n
            return tok_lntok, sX.n

        xt_free2 = [0, 0]
        ln_free2 = [0, 0]
        for t in range(TT):
            slot = t % 2

            @block.sync
            def _(sy):
                if xt_free2[slot]:
                    sy.wait_ge(sV.h, xt_free2[slot])
                sy.dma_start(out=xt2[slot], in_=x_in[t * P:(t + 1) * P, :]).then_inc(sX.h, 16)
                sX.n += 16
            tok_load = sX.n
            xt_free2[slot], ln_free2[slot] = emit_ln_tile(
                t, tok_load, want_rms=False, lnfree=(sX, ln_free2[slot]),
                xt_ap=xt2[slot], ln_ap=ln2v[slot])
        lnT_all = lnT_tile_tok[TT - 1]

        def dump_featmajor(tap_ap, buf, ready_sem, ready_tok):
            for c in range(NCH):
                @block.vector
                def _(v):
                    v.wait_ge(ready_sem.h, ready_tok)
                    v.wait_ge(sX.h, sX.n)
                    sV.inc(v.tensor_copy(ostage, buf[:, c, :]))
                vt = sV.n

                @block.sync
                def _(sy):
                    sy.wait_ge(sV.h, vt)
                    sy.dma_start(out=tap_ap[c * P:(c + 1) * P, :], in_=ostage).then_inc(sX.h, 16)
                    sX.n += 16

        if "lnT" in taps:
            dump_featmajor(tap_d["tap_lnT"], lnT, sX, lnT_all)

        wpan_free = [(sPE, 0), (sPE, 0)]
        ps_free = [(sV, 0), (sV, 0)]

        def emit_panel_load(w_ap, m, nkb, slot):
            fs, ft = wpan_free[slot]
            sw = sW0 if slot == 0 else sW1

            @block.gpsimd
            def _(g):
                if ft:
                    g.wait_ge(fs.h, ft)
                g.dma_start(
                    out=wpan[:, slot, 0:nkb, :],
                    in_=w_ap.rearrange("(kb p) m -> p kb m", p=P)[:, :, m * P:(m + 1) * P],
                ).then_inc(sw.h, 16)
                sw.n += 16
            return (sw, sw.n)

        def emit_proj_mm(slot, nkb, wtok, rhs3d, extra_waits=()):
            ps = ps_s if slot == 0 else ps_x
            fs, ft = ps_free[slot]

            @block.tensor
            def _(te):
                te.wait_ge(wtok[0].h, wtok[1])
                for s_, tv in extra_waits:
                    if tv:
                        te.wait_ge(s_.h, tv)
                if ft:
                    te.wait_ge(fs.h, ft)
                instr = None
                for kb in range(nkb):
                    for half in range(2):
                        instr = te.matmul(ps[:, half * 512:(half + 1) * 512],
                                          wpan[:, slot, kb, :],
                                          rhs3d[:, kb, half * 512:(half + 1) * 512],
                                          start=(kb == 0), stop=(kb == nkb - 1))
                sPE.inc(instr)
            wpan_free[slot] = (sPE, sPE.n)
            return sPE.n, ps

        kv_store_tok = 0
        evac_free = [(sX, 0), (sX, 0)]
        for m in range(NKV):
            slot = m % 2
            wtok = emit_panel_load(wkT, m, NCH, slot)
            mmtok, ps = emit_proj_mm(slot, NCH, wtok, lnT, extra_waits=[(sX, lnT_all)])

            @block.vector
            def _(v):
                v.wait_ge(sPE.h, mmtok)
                efs, eft = evac_free[slot]
                if eft:
                    v.wait_ge(efs.h, eft)
                sV.inc(v.tensor_copy(evac[:, slot], ps[:, :]))
            ps_free[slot] = (sV, sV.n)
            etok = sV.n

            @block.sync
            def _(sy):
                sy.wait_ge(sV.h, etok)
                sy.dma_start(out=kv_own.ap()[m * P:(m + 1) * P, :], in_=evac[:, slot]).then_inc(sX.h, 16)
                sX.n += 16
            evac_free[slot] = (sX, sX.n)
            kv_store_tok = sX.n

        for m in range(NKV):
            slot = m % 2
            wtok = emit_panel_load(wvT, m, NCH, slot)
            mmtok, ps = emit_proj_mm(slot, NCH, wtok, lnT)

            @block.vector
            def _(v):
                v.wait_ge(sPE.h, mmtok)
                efs, eft = evac_free[slot]
                if eft:
                    v.wait_ge(efs.h, eft)
                v.wait_ge(sX.h, tok_consts)
                sV.inc(v.tensor_scalar(out=evac[:, slot], in0=ps[:, :],
                                       scalar1=bv_sb[:, m:m + 1], scalar2=None, op0=ALU.add))
            ps_free[slot] = (sV, sV.n)
            etok = sV.n

            @block.sync
            def _(sy):
                sy.wait_ge(sV.h, etok)
                for tt in range(TT):
                    sy.dma_start_transpose(
                        out=v_stage[:, tt, m * P:(m + 1) * P],
                        in_=evac[:, slot, tt * P:(tt + 1) * P]).then_inc(sX.h, 16)
                    sX.n += 16
            evac_free[slot] = (sX, sX.n)
        vstage_tok = sX.n

        @block.sync
        def _(sy):
            sy.wait_ge(sX.h, vstage_tok)
            sy.dma_start(out=kv_own.ap()[1024:2048, :].rearrange("(t p) f -> p t f", p=P),
                         in_=v_stage[:, :, :]).then_inc(sX.h, 16)
            sX.n += 16
        kv_store_tok = sX.n

        @block.gpsimd
        def _(g):
            g.wait_ge(sX.h, kv_store_tok)
            sG.inc(g.collective_compute(
                "AllGather", ALU.bypass,
                replica_groups=[[0, 1, 2, 3], [4, 5, 6, 7]],
                ins=[kv_own.ap().opt()],
                outs=[kv_gath.ap().opt()],
            ))
        tok_gath = sG.n

        ps_q = ps_tr[:, :].bitcast(F32)
        q_stok = [0] * NCH
        ps_q_free = [(sV, 0)]

        def emit_q_chunk(m):
            slot = m % 2
            wtok = emit_panel_load(wqT, m, NCH, slot)

            @block.tensor
            def _(te):
                te.wait_ge(wtok[0].h, wtok[1])
                qfs, qft = ps_q_free[0]
                if qft:
                    te.wait_ge(qfs.h, qft)
                instr = None
                for kb in range(NCH):
                    for hf in range(2):
                        instr = te.matmul(ps_q[:, hf * 512:(hf + 1) * 512],
                                          wpan[:, slot, kb, :],
                                          lnT[:, kb, hf * 512:(hf + 1) * 512],
                                          start=(kb == 0), stop=(kb == NCH - 1))
                sPE.inc(instr)
            wpan_free[slot] = (sPE, sPE.n)
            mmtok = sPE.n

            @block.vector
            def _(v):
                v.wait_ge(sPE.h, mmtok)
                efs, eft = evac_free[slot]
                if eft:
                    v.wait_ge(efs.h, eft)
                sV.inc(v.tensor_copy(evac[:, slot], ps_q[:, :]))
            ps_q_free[0] = (sV, sV.n)
            etok = sV.n

            @block.sync
            def _(sy):
                sy.wait_ge(sV.h, etok)
                sy.dma_start(out=q_d[m * P:(m + 1) * P, :], in_=evac[:, slot]).then_inc(sX.h, 16)
                sX.n += 16
            evac_free[slot] = (sX, sX.n)
            q_stok[m] = sX.n

        def dump_dram_bf16(tap_ap, src_ap, nrow, ready_tok_sem, ready_tok):
            for r in range(nrow // P):
                @block.sync
                def _(sy):
                    sy.wait_ge(ready_tok_sem.h, ready_tok)
                    sy.wait_ge(sX.h, sX.n)
                    sy.dma_start(out=evac[:, 0], in_=src_ap[r * P:(r + 1) * P, :]).then_inc(sX.h, 16)
                    sX.n += 16
                lt = sX.n

                @block.vector
                def _(v):
                    v.wait_ge(sX.h, lt)
                    sV.inc(v.tensor_copy(ostage, evac[:, 0]))
                vt = sV.n

                @block.sync
                def _(sy):
                    sy.wait_ge(sV.h, vt)
                    sy.dma_start(out=tap_ap[r * P:(r + 1) * P, :], in_=ostage).then_inc(sX.h, 16)
                    sX.n += 16

        xt_bf = xt[:].bitcast(BF16)
        K_h2 = [K_h, ln_tok[:, 0:4096].rearrange("p (r t) -> p r t", r=4)]
        V_sb2 = [V_sb, xt_bf[:, 0:4352].rearrange("p (c f) -> p c f", c=KT)]
        qbuf2 = [qbuf[:, :], xt_bf[:, 4352:5376]]

        @block.gpsimd
        def _(g):
            g.wait_ge(sX.h, kv_store_tok)
            sG.inc(g.memset(V_sb2[0][:, :, 128:129], 1.0))
            g.wait_ge(sX.h, lnT_all)
            sG.inc(g.memset(V_sb2[1][:, :, 128:129], 1.0))
            g.wait_ge(sG.h, sG.n)
        tok_ones = sG.n

        NPAIR = KT // 2
        kh_busy_slot = [0, 0]
        tok_kvq = [0] * NH

        def emit_head_load(h):
            kv = h // 4
            slot = h % 2

            @block.sync
            def _(sy):
                if kh_busy_slot[slot]:
                    sy.wait_ge(sPE.h, kh_busy_slot[slot])
                if h <= 1:
                    sy.wait_ge(sG.h, tok_gath)
                    sy.wait_ge(sG.h, tok_ones)
                sy.wait_ge(sX.h, q_stok[h])
                sy.dma_start(
                    out=K_h2[slot][:, :, :],
                    in_=kv_gath.ap().rearrange("(r a) t -> a r t", r=4)[kv * P:(kv + 1) * P],
                ).then_inc(sX.h, 16)
                for r in range(4):
                    sy.dma_start(
                        out=V_sb2[slot][:, r * 8:(r + 1) * 8, 0:128],
                        in_=kv_gath.ap().rearrange("(q s p) f -> q p s f", s=8, p=P)[2 * r + 1][:, :, kv * P:(kv + 1) * P],
                    ).then_inc(sX.h, 16)
                sy.dma_start(out=qbuf2[slot], in_=q_d[h * P:(h + 1) * P, :]).then_inc(sX.h, 16)
                sX.n += 96
            tok_kvq[h] = sX.n

        sc5_base = sSC.n
        order5 = []
        for p_ in range(NPAIR):
            order5.append(("mm", p_))
            if p_ >= 1:
                order5.append(("pv", p_ - 1))
        order5.append(("pv", NPAIR - 1))

        pe5_base = sPE.n

        def half_sched(h, half):
            k = 2 * h + half
            hb = pe5_base + min(h + 3, NCH) + 32 * k
            mm_tok, pv_tok = {}, {}
            ntok = hb
            for kind, p_ in order5:
                ntok += 1
                (mm_tok if kind == "mm" else pv_tok)[p_] = ntok
            sb = sc5_base + k * NPAIR
            exp_tok = {p_: sb + p_ + 1 for p_ in range(NPAIR)}
            return mm_tok, pv_tok, None, exp_tok

        state5 = {"ps_ctx_free": 0, "ctx_st_free": [0, 0, 0, 0]}

        def emit_attn_head(h):
            slot = h % 2
            Kh = K_h2[slot]
            Vh = V_sb2[slot]
            Qh = qbuf2[slot]
            for half in range(2):
                mm_tok, pv_tok, _tr, exp_tok = half_sched(h, half)
                if (h, half) > (0, 0):
                    pexp = half_sched(*((h, 0) if half == 1 else (h - 1, 1)))[3]
                    prev_exp_last2 = (pexp[NPAIR - 2], pexp[NPAIR - 1])
                else:
                    prev_exp_last2 = (0, 0)
                prev_psfree = state5["ps_ctx_free"]

                @block.tensor
                def _(te):
                    te.wait_ge(sX.h, tok_kvq[h])
                    dummies_done = False
                    for kind, p_ in order5:
                        if kind == "mm":
                            if p_ >= 2:
                                te.wait_ge(sSC.h, exp_tok[p_ - 2])
                            elif prev_exp_last2[p_]:
                                te.wait_ge(sSC.h, prev_exp_last2[p_])
                            ps = ps_s if p_ % 2 == 0 else ps_x
                            for cl in range(2):
                                c = 2 * p_ + cl
                                instr = te.matmul(ps[:, cl * 512:(cl + 1) * 512],
                                                  Kh[:, c // 8, (c % 8) * P:(c % 8 + 1) * P],
                                                  Qh[:, half * 512:(half + 1) * 512],
                                                  start=True, stop=True)
                            sPE.inc(instr)
                            assert sPE.n == mm_tok[p_], (h, half, p_, sPE.n, mm_tok[p_])
                        else:
                            if not dummies_done:
                                if prev_psfree:
                                    te.wait_ge(sV.h, prev_psfree)
                                te.matmul(ps_ctx[:, 496:512], ident[:], ident[:, 0:16],
                                          start=True, stop=True)
                                te.matmul(ps_ctx[:, 1008:1024], ident[:], ident[:, 0:16],
                                          start=True, stop=True)
                                dummies_done = True
                            te.wait_ge(sSC.h, exp_tok[p_])
                            for cl in range(2):
                                c = 2 * p_ + cl
                                for qt in range(4):
                                    instr = te.matmul(ps_ctx[:, qt * 256: qt * 256 + 129],
                                                      pbuf[:, p_ % 2, cl * 512 + qt * P: cl * 512 + (qt + 1) * P],
                                                      Vh[:, c, 0:129],
                                                      start=False, stop=(c == KT - 1),
                                                      skip_group_check=True)
                            sPE.inc(instr)
                            assert sPE.n == pv_tok[p_], (h, half, p_, sPE.n, pv_tok[p_])
                kh_busy_slot[slot] = pv_tok[NPAIR - 1]

                @block.vector
                def _(v):
                    v.wait_ge(sPE.h, pv_tok[NPAIR - 1])
                    if state5["ctx_st_free"][0]:
                        v.wait_ge(sX.h, max(state5["ctx_st_free"]))
                    for qt in range(4):
                        sV.inc(v.reciprocal(rcp[:, qt:qt + 1],
                                            ps_ctx[:, qt * 256 + 128: qt * 256 + 129]))
                    v.wait_ge(sV.h, sV.n)
                    for qt in range(4):
                        sV.inc(v.tensor_scalar(out=ctx_st[:, qt, :],
                                               in0=ps_ctx[:, qt * 256: qt * 256 + 128],
                                               scalar1=rcp[:, qt:qt + 1], scalar2=None,
                                               op0=ALU.mult))
                stok = sV.n
                state5["ps_ctx_free"] = stok

                @block.sync
                def _(sy):
                    sy.wait_ge(sV.h, stok)
                    for qt in range(4):
                        sy.dma_start_transpose(
                            out=ctxT[:, h, half * 512 + qt * P: half * 512 + (qt + 1) * P],
                            in_=ctx_st[:, qt, :]).then_inc(sX.h, 16)
                        sX.n += 16
                for qt in range(4):
                    state5["ctx_st_free"][qt] = sX.n

        for m in range(NCH):
            emit_q_chunk(m)
            if m >= 2:
                emit_attn_head(m - 2)
            emit_head_load(m)
        emit_attn_head(NH - 2)
        emit_attn_head(NH - 1)
        ctx_evac_tok = sX.n
        q_store_tok = q_stok[NCH - 1]

        @block.scalar
        def _(s):
            for h in range(NH):
                for half in range(2):
                    mm_tok, pv_tok, tr_tok, exp_tok = half_sched(h, half)
                    if (h, half) > (0, 0):
                        pmm, ppv, ptr, pexp = half_sched(*((h, 0) if half == 1 else (h - 1, 1)))
                        prev_pv_last2 = (ppv[NPAIR - 2], ppv[NPAIR - 1])
                    else:
                        prev_pv_last2 = (0, 0)
                    for p_ in range(NPAIR):
                        s.wait_ge(sPE.h, mm_tok[p_])
                        if p_ >= 2:
                            s.wait_ge(sPE.h, pv_tok[p_ - 2])
                        elif prev_pv_last2[p_]:
                            s.wait_ge(sPE.h, prev_pv_last2[p_])
                        ps = ps_s if p_ % 2 == 0 else ps_x
                        sSC.inc(s.activation(out=pbuf[:, p_ % 2, :],
                                             in_=ps[:, :],
                                             func=AF.Exp, scale=SCALE, bias=cb[:, 2:3]))
        assert sSC.n == sc5_base + NH * 2 * NPAIR
        ctxT_all = ctx_evac_tok

        if "ctxT" in taps:
            dump_featmajor(tap_d["tap_ctxT"], ctxT, sX, ctxT_all)

        aoT_free = [(sACC, 0)] * 6
        acc_queue = []
        def emit_accum(m, aslot, vt):
            @block.gpsimd
            def _(g):
                g.wait_ge(sX.h, vt)
                g.wait_ge(sPC.h, tok_x2pre)
                g.dma_start(
                    out=x2_d.rearrange("(t p) c -> p t c", p=P)[:, :, m * P:(m + 1) * P],
                    in_=aoT[:, aslot], accum_op=ALU.add,
                ).then_inc(sACC.h, 16)
                sACC.n += 16
            aoT_free[aslot] = (sACC, sACC.n)

        for m in range(NCH):
            slot = m % 2
            wtok = emit_panel_load(woT, m, NCH, slot)
            if len(acc_queue) >= 6:
                emit_accum(*acc_queue.pop(0))
            mmtok, ps = emit_proj_mm(slot, NCH, wtok, ctxT, extra_waits=[(sX, ctxT_all)])

            @block.vector
            def _(v):
                v.wait_ge(sPE.h, mmtok)
                efs, eft = evac_free[slot]
                if eft:
                    v.wait_ge(efs.h, eft)
                sV.inc(v.tensor_copy(evac[:, slot], ps[:, :]))
            ps_free[slot] = (sV, sV.n)
            etok = sV.n

            aslot = m % 6
            @block.sync
            def _(sy):
                sy.wait_ge(sV.h, etok)
                afs, aft = aoT_free[aslot]
                if aft:
                    sy.wait_ge(afs.h, aft)
                for tt in range(TT):
                    sy.dma_start_transpose(
                        out=aoT[:, aslot, tt, :],
                        in_=evac[:, slot, tt * P:(tt + 1) * P]).then_inc(sX.h, 16)
                    sX.n += 16
            evac_free[slot] = (sX, sX.n)
            acc_queue.append((m, aslot, sX.n))
        for item in acc_queue:
            emit_accum(*item)
        x2_done = sACC.n

        logit_psfree = 0
        xpb_free = (sV, 0)
        xpre_done = 0
        xt_free2 = [0, 0]
        ln_free2 = [0, 0]
        for t in range(TT):
            slot = t % 2

            @block.sync
            def _(sy):
                if xt_free2[slot]:
                    sy.wait_ge(sV.h, xt_free2[slot])
                sy.wait_ge(sACC.h, x2_done)
                sy.dma_start(out=xt2[slot], in_=x2_d[t * P:(t + 1) * P, :]).then_inc(sX.h, 16)
                sX.n += 16
            tok_load = sX.n

            xt_sc_tok, lnt_pe_tok = emit_ln_tile(
                t, tok_load, want_rms=True, lnfree=(sX, ln_free2[slot]),
                xt_ap=xt2[slot], ln_ap=ln2v[slot])
            xt_free2[slot] = xt_sc_tok

            @block.tensor
            def _(te):
                te.wait_ge(sX.h, lnT_tile_tok[t])
                te.wait_ge(sW.h, tok_wlog)
                if logit_psfree:
                    te.wait_ge(sV.h, logit_psfree)
                instr = None
                for c in range(NCH):
                    instr = te.matmul(ps_ctx[:, 0:24],
                                      lnT[:, c, t * P:(t + 1) * P],
                                      wlog_sb[:, c, :],
                                      start=(c == 0), stop=(c == NCH - 1))
                sPE.inc(instr)
            tok_logit = sPE.n

            @block.vector
            def _(v):
                v.wait_ge(sPE.h, tok_logit)
                v.wait_ge(sV.h, tok_bases)
                sV.inc(v.tensor_scalar(out=Z[:, t], in0=ps_ctx[:, 0:16],
                                       scalar1=rms[:, t:t + 1], scalar2=10.0,
                                       op0=ALU.mult, op1=ALU.mult))
                sV.inc(v.tensor_scalar(out=PL[:, t, 0:4], in0=ps_ctx[:, 16:20],
                                       scalar1=rms[:, t:t + 1], scalar2=None, op0=ALU.mult))
                sV.inc(v.tensor_scalar(out=PL[:, t, 4:8], in0=ps_ctx[:, 20:24],
                                       scalar1=rms[:, t:t + 1], scalar2=None, op0=ALU.mult))
                v.wait_ge(sV.h, sV.n)
                sV.inc(v.tensor_tensor(out=Z[:, t], in0=Z[:, t], in1=bases[:, 0:16], op=ALU.add))
                sV.inc(v.tensor_tensor(out=PL[:, t, 0:4], in0=PL[:, t, 0:4],
                                       in1=bases[:, 16:20], op=ALU.add))
                sV.inc(v.tensor_tensor(out=PL[:, t, 4:8], in0=PL[:, t, 4:8],
                                       in1=bases[:, 20:24], op=ALU.add))
            logit_psfree = sV.n

            @block.vector
            def _(v):
                v.wait_ge(sV.h, logit_psfree)
                sV.inc(v.reduce_max(hps[:, t, 0:1], PL[:, t, 0:4], axis=AX.X))
                v.wait_ge(sV.h, sV.n)
                sV.inc(v.tensor_scalar(out=rsb[:, t], in0=PL[:, t, 0:4],
                                       scalar1=hps[:, t, 0:1], scalar2=None, op0=ALU.subtract))
            tok_hp1 = sV.n

            @block.scalar
            def _(s):
                s.wait_ge(sV.h, tok_hp1)
                sSC.inc(s.activation(out=lsb[:, t], in_=rsb[:, t], func=AF.Exp))
            tok_hp2 = sSC.n

            @block.vector
            def _(v):
                v.wait_ge(sSC.h, tok_hp2)
                sV.inc(v.reduce_sum(hps[:, t, 1:2], lsb[:, t], axis=AX.X))
                v.wait_ge(sV.h, sV.n)
                sV.inc(v.reciprocal(hps[:, t, 2:3], hps[:, t, 1:2]))
                v.wait_ge(sV.h, sV.n)
                sV.inc(v.tensor_scalar(out=Hpre[:, t], in0=lsb[:, t],
                                       scalar1=hps[:, t, 2:3], scalar2=None, op0=ALU.mult))
            tok_hpre = sV.n

            @block.vector
            def _(v):
                v.wait_ge(sV.h, tok_hpre)
                v.wait_ge(sV.h, xt_sc_tok)
                fs2, ft2 = xpb_free
                if ft2:
                    v.wait_ge(fs2.h, ft2)
                lnv = ln2v[t % 2]
                sV.inc(v.tensor_scalar(out=ostage, in0=lnv[:, 0:1024],
                                       scalar1=Hpre[:, t, 0:1], scalar2=None, op0=ALU.mult))
                for j in range(1, 4):
                    v.wait_ge(sV.h, sV.n)
                    sV.inc(v.scalar_tensor_tensor(out=ostage, in0=lnv[:, j * 1024:(j + 1) * 1024],
                                                  scalar=Hpre[:, t, j:j + 1], in1=ostage,
                                                  op0=ALU.mult, op1=ALU.add))
                v.wait_ge(sV.h, sV.n)
                sV.inc(v.tensor_copy(xpb, ostage))
            tok_xpb = sV.n

            @block.sync
            def _(sy):
                sy.wait_ge(sV.h, tok_xpb)
                for cj in range(8):
                    sy.dma_start_transpose(
                        out=x_preT[:, cj, t * P:(t + 1) * P],
                        in_=xpb[:, cj * P:(cj + 1) * P]).then_inc(sX.h, 16)
                    sX.n += 16
            xpb_free = (sX, sX.n)
            ln_free2[t % 2] = max(lnt_pe_tok, sX.n)
        xpre_done = sX.n
        z_done = sV.n

        ybr_done = 0
        for co in range(8):
            slot = co % 2
            wtok = emit_panel_load(wbr, co, 8, slot)
            mmtok, ps = emit_proj_mm(slot, 8, wtok, x_preT, extra_waits=[(sX, xpre_done)])

            @block.vector
            def _(v):
                v.wait_ge(sPE.h, mmtok)
                efs, eft = evac_free[slot]
                if eft:
                    v.wait_ge(efs.h, eft)
                sV.inc(v.tensor_copy(evac[:, slot], ps[:, :]))
            ps_free[slot] = (sV, sV.n)
            etok = sV.n

            @block.sync
            def _(sy):
                sy.wait_ge(sV.h, etok)
                if co == 0:
                    sy.wait_ge(sPE.h, max(kh_busy_slot))
                for tt in range(TT):
                    sy.dma_start_transpose(
                        out=y_all[:, tt, co * P:(co + 1) * P],
                        in_=evac[:, slot, tt * P:(tt + 1) * P]).then_inc(sX.h, 16)
                    sX.n += 16
            evac_free[slot] = (sX, sX.n)
        ybr_done_x = sX.n

        sc_base = sSC.n
        v_base = sV.n
        NIT2 = 2 * N_ITER
        vt = lambda k, i: v_base + 10 * k + i
        st = lambda k, i: sc_base + 4 * k + i

        def gviews(g):
            sl = slice(4 * g, 4 * (g + 1))
            return dict(
                Zf=Z[:, sl, :].rearrange("p t e -> p (t e)"),
                Zsf=Zs[:, sl, :].rearrange("p t e -> p (t e)"),
                Ef=Ebuf[:, sl, :].rearrange("p t e -> p (t e)"),
                Z4r=Z[:, sl, :].rearrange("p t (i j) -> p t i j", j=4),
                Z4c=Z[:, sl, :].rearrange("p t (i j) -> p t j i", j=4),
                Zs4r=Zs[:, sl, :].rearrange("p t (i j) -> p t i j", j=4),
                Zs4c=Zs[:, sl, :].rearrange("p t (i j) -> p t j i", j=4),
                E4r=Ebuf[:, sl, :].rearrange("p t (i j) -> p t i j", j=4),
                E4c=Ebuf[:, sl, :].rearrange("p t (i j) -> p t j i", j=4),
                mx=mxb[:, sl, :], rs=rsb[:, sl, :], ls=lsb[:, sl, :],
            )
        GV = [gviews(0), gviews(1)]

        @block.scalar
        def _(s):
            for k in range(NIT2):
                for g in range(2):
                    s.wait_ge(sV.h, vt(k, 2 + 2 * g))
                    sSC.inc(s.activation(out=GV[g]["Ef"], in_=GV[g]["Zsf"], func=AF.Exp))
                    assert sSC.n == st(k, 1 + g)
                for g in range(2):
                    s.wait_ge(sV.h, vt(k, 5 + g))
                    sSC.inc(s.activation(out=GV[g]["ls"], in_=GV[g]["rs"], func=AF.Ln))
                    assert sSC.n == st(k, 3 + g)
            s.wait_ge(sV.h, vt(NIT2 - 1, 8))
            sSC.inc(s.activation(out=GV[0]["Ef"], in_=GV[0]["Zf"], func=AF.Exp))
            s.wait_ge(sV.h, vt(NIT2 - 1, 10))
            sSC.inc(s.activation(out=GV[1]["Ef"], in_=GV[1]["Zf"], func=AF.Exp))
            sSC.inc(s.activation(out=gtmp[:, 0:1], in_=gate_sb[:, 0:1], func=AF.Exp, scale=-1.0))
        assert sSC.n == sc_base + 4 * NIT2 + 3
        tok_hres = sc_base + 4 * NIT2 + 2
        tok_eg = sSC.n

        @block.vector
        def _(v):
            v.wait_ge(sV.h, z_done)
            for k in range(NIT2):
                rows = (k % 2 == 0)
                for g in range(2):
                    Zv = GV[g]["Z4r"] if rows else GV[g]["Z4c"]
                    Zsv = GV[g]["Zs4r"] if rows else GV[g]["Zs4c"]
                    if k > 0:
                        v.wait_ge(sV.h, vt(k - 1, 8 + 2 * g))
                    sV.inc(v.reduce_max(GV[g]["mx"], Zv, axis=AX.X))
                    v.wait_ge(sV.h, sV.n)
                    mB = GV[g]["mx"].unsqueeze(3).broadcast_to([P, 4, 4, 4])
                    sV.inc(v.tensor_tensor(out=Zsv, in0=Zv, in1=mB, op=ALU.subtract))
                    assert sV.n == vt(k, 2 + 2 * g)
                for g in range(2):
                    Ev = GV[g]["E4r"] if rows else GV[g]["E4c"]
                    v.wait_ge(sSC.h, st(k, 1 + g))
                    sV.inc(v.reduce_sum(GV[g]["rs"], Ev, axis=AX.X))
                    assert sV.n == vt(k, 5 + g)
                for g in range(2):
                    Zv = GV[g]["Z4r"] if rows else GV[g]["Z4c"]
                    v.wait_ge(sSC.h, st(k, 3 + g))
                    sV.inc(v.tensor_tensor(out=GV[g]["ls"], in0=GV[g]["ls"],
                                           in1=GV[g]["mx"], op=ALU.add))
                    v.wait_ge(sV.h, sV.n)
                    lsB = GV[g]["ls"].unsqueeze(3).broadcast_to([P, 4, 4, 4])
                    sV.inc(v.tensor_tensor(out=Zv, in0=Zv, in1=lsB, op=ALU.subtract))
                    assert sV.n == vt(k, 8 + 2 * g)
        assert sV.n == v_base + 10 * NIT2

        @block.vector
        def _(v):
            v.wait_ge(sSC.h, tok_eg)
            sV.inc(v.tensor_scalar(out=gtmp[:, 0:1], in0=gtmp[:, 0:1], scalar1=1.0,
                                   scalar2=None, op0=ALU.add))
            v.wait_ge(sV.h, sV.n)
            sV.inc(v.reciprocal(gtmp[:, 1:2], gtmp[:, 0:1]))
            v.wait_ge(sV.h, sV.n)
            sV.inc(v.tensor_scalar(out=gtmp[:, 0:1], in0=gtmp[:, 1:2], scalar1=-1.0,
                                   scalar2=1.0, op0=ALU.mult, op1=ALU.add))
        tok_g = sV.n

        @block.vector
        def _(v):
            v.wait_ge(sV.h, logit_psfree)
            sV.inc(v.reduce_max(hsc[:, :, 0:1], PL[:, :, 4:8], axis=AX.X))
            v.wait_ge(sV.h, sV.n)
            sV.inc(v.tensor_tensor(out=Hpost[:, :, :], in0=PL[:, :, 4:8],
                                   in1=hsc[:, :, 0:1].broadcast_to([P, TT, 4]), op=ALU.subtract))
        tok_po1 = sV.n

        @block.scalar
        def _(s):
            s.wait_ge(sV.h, tok_po1)
            sSC.inc(s.activation(out=Hpost[:, :, :], in_=Hpost[:, :, :], func=AF.Exp))
        tok_po2 = sSC.n

        @block.vector
        def _(v):
            v.wait_ge(sSC.h, tok_po2)
            sV.inc(v.reduce_sum(hsc[:, :, 1:2], Hpost[:, :, :], axis=AX.X))
            v.wait_ge(sV.h, sV.n)
            sV.inc(v.reciprocal(hsc[:, :, 0:1], hsc[:, :, 1:2]))
            v.wait_ge(sV.h, sV.n)
            sV.inc(v.tensor_tensor(out=Hpost[:, :, :], in0=Hpost[:, :, :],
                                   in1=hsc[:, :, 0:1].broadcast_to([P, TT, 4]), op=ALU.mult))
        tok_hpost = sV.n

        @block.vector
        def _(v):
            v.wait_ge(sSC.h, tok_hres)
            v.wait_ge(sV.h, tok_g)
            for t in range(TT):
                sV.inc(v.tensor_tensor(out=gtmp[:, 2:3], in0=gtmp[:, 1:2],
                                       in1=rstd[:, t:t + 1], op=ALU.mult))
                sV.inc(v.tensor_tensor(out=gtmp[:, 3:4], in0=gtmp[:, 0:1],
                                       in1=rstd[:, t:t + 1], op=ALU.mult))
                v.wait_ge(sV.h, sV.n)
                sV.inc(v.tensor_scalar(out=coef[:, t], in0=Ebuf[:, t],
                                       scalar1=gtmp[:, 2:3], scalar2=None, op0=ALU.mult))
                v.wait_ge(sV.h, sV.n)
                for i in range(4):
                    sV.inc(v.tensor_scalar(out=coef[:, t, 5 * i:5 * i + 1],
                                           in0=coef[:, t, 5 * i:5 * i + 1],
                                           scalar1=gtmp[:, 3:4], scalar2=None, op0=ALU.add))
                v.wait_ge(sV.h, sV.n)
                sV.inc(v.reduce_sum(nb[:, t], coef[:, t].rearrange("p (i j) -> p i j", j=4),
                                    axis=AX.X))
                v.wait_ge(sV.h, sV.n)
                sV.inc(v.tensor_scalar(out=nb[:, t], in0=nb[:, t],
                                       scalar1=mv[:, t, 0:1], scalar2=-1.0,
                                       op0=ALU.mult, op1=ALU.mult))
        tok_coef = sV.n

        if "ffn" in taps:
            @block.vector
            def _(v):
                v.wait_ge(sV.h, tok_coef)
                v.wait_ge(sV.h, tok_hpost)
                sV.inc(v.tensor_copy(ostage[:, 0:16], coef[:, 0]))
                sV.inc(v.tensor_copy(ostage[:, 16:20], Hpre[:, 0]))
                sV.inc(v.tensor_copy(ostage[:, 20:24], Hpost[:, 0]))
                sV.inc(v.tensor_copy(ostage[:, 24:28], nb[:, 0]))
                sV.inc(v.tensor_copy(ostage[:, 28:44], Z[:, 0]))
                sV.inc(v.tensor_copy(ostage[:, 44:60], Ebuf[:, 0]))
            vt = sV.n

            @block.sync
            def _(sy):
                sy.wait_ge(sV.h, vt)
                sy.dma_start(out=tap_d["tap_ffn"][:, 0:96], in_=ostage[:, 0:96]).then_inc(sX.h, 16)
                sX.n += 16
            dump_dram_bf16
            for tt_ in range(TT):
                @block.vector
                def _(v):
                    v.wait_ge(sX.h, ybr_done_x)
                    v.wait_ge(sX.h, sX.n)
                    sV.inc(v.tensor_copy(ostage, y_all[:, tt_, :]))
                vt2 = sV.n

                @block.sync
                def _(sy):
                    sy.wait_ge(sV.h, vt2)
                    sy.dma_start(out=tap_d["tap_y"][tt_ * P:(tt_ + 1) * P, :], in_=ostage).then_inc(sX.h, 16)
                    sX.n += 16

        mix_xt_free = (sV, xt_free2[0])
        for t in range(TT):
            @block.sync
            def _(sy):
                fs, ft = mix_xt_free
                if ft:
                    sy.wait_ge(fs.h, ft)
                sy.dma_start(out=xt[:], in_=x2_d[t * P:(t + 1) * P, :]).then_inc(sX.h, 16)
                sX.n += 16
            tok_load = sX.n

            last_mix_tok = 0
            for i in range(4):
                osl = (t * 4 + i) % 2
                ost = ostage2[osl]
                prev_store = sOUT.n - 16

                @block.vector
                def _(v):
                    v.wait_ge(sX.h, tok_load)
                    v.wait_ge(sV.h, tok_coef)
                    v.wait_ge(sV.h, tok_hpost)
                    v.wait_ge(sX.h, ybr_done_x)
                    if prev_store > 0:
                        v.wait_ge(sOUT.h, prev_store)
                    sV.inc(v.scalar_tensor_tensor(out=ost, in0=xt[:, 0:1024],
                                                  scalar=coef[:, t, 4 * i:4 * i + 1],
                                                  in1=xt[:, i * 1024:(i + 1) * 1024],
                                                  op0=ALU.mult, op1=ALU.add))
                    for j in range(1, 4):
                        v.wait_ge(sV.h, sV.n)
                        sV.inc(v.scalar_tensor_tensor(out=ost,
                                                      in0=xt[:, j * 1024:(j + 1) * 1024],
                                                      scalar=coef[:, t, 4 * i + j:4 * i + j + 1],
                                                      in1=ost,
                                                      op0=ALU.mult, op1=ALU.add))
                    v.wait_ge(sV.h, sV.n)
                    sV.inc(v.scalar_tensor_tensor(out=ost, in0=y_all[:, t, :],
                                                  scalar=Hpost[:, t, i:i + 1], in1=ost,
                                                  op0=ALU.mult, op1=ALU.add))
                    v.wait_ge(sV.h, sV.n)
                    sV.inc(v.tensor_scalar(out=ost, in0=ost,
                                           scalar1=nb[:, t, i:i + 1], scalar2=None, op0=ALU.add))
                mtok = sV.n
                last_mix_tok = mtok

                @block.sync
                def _(sy):
                    sy.wait_ge(sV.h, mtok)
                    sy.dma_start(out=out_d[t * P:(t + 1) * P, i * 1024:(i + 1) * 1024],
                                 in_=ost).then_inc(sOUT.h, 16)
                    sOUT.n += 16
            mix_xt_free = (sV, last_mix_tok)

        @block.sync
        def _(sy):
            sy.wait_ge(sOUT.h, sOUT.n)

    return nc


_PROG_CACHE = {}


def _get_prog(taps=()):
    key = tuple(sorted(taps))
    if key not in _PROG_CACHE:
        _PROG_CACHE[key] = build_program(key)
    return _PROG_CACHE[key]


def make_in_maps(inputs):
    x = np.asarray(inputs["L0_attn_hidden_states"], dtype=np.float32)
    wq = np.asarray(inputs["L0_attn_Wq"], dtype=np.float32)
    wk = np.asarray(inputs["L0_attn_Wk"], dtype=np.float32)
    wv = np.asarray(inputs["L0_attn_Wv"], dtype=np.float32)
    bv = np.asarray(inputs["L0_attn_bv"], dtype=np.float32)
    wo = np.asarray(inputs["L0_attn_Wo"], dtype=np.float32)
    w_res = np.asarray(inputs["L0_ffn_W_res"], dtype=np.float32)
    w_pre = np.asarray(inputs["L0_ffn_W_pre"], dtype=np.float32)
    w_post = np.asarray(inputs["L0_ffn_W_post"], dtype=np.float32)
    hres_b = np.asarray(inputs["L0_ffn_H_res_base"], dtype=np.float32)
    hpre_b = np.asarray(inputs["L0_ffn_H_pre_base"], dtype=np.float32)
    hpost_b = np.asarray(inputs["L0_ffn_H_post_base"], dtype=np.float32)
    gate = np.asarray(inputs["L0_ffn_gate"], dtype=np.float32)
    wbr_ = np.asarray(inputs["L0_ffn_W_branch"], dtype=np.float32)

    shared = {
        "wqT": np.ascontiguousarray(wq.T),
        "wkT": np.ascontiguousarray(wk.T),
        "wvT": np.ascontiguousarray(wv.T),
        "woT": np.ascontiguousarray(wo.T),
        "wbr": np.ascontiguousarray(wbr_),
        "wlogT": np.ascontiguousarray(np.concatenate([w_res, w_pre, w_post], axis=0).T),
        "bases_in": np.ascontiguousarray(np.broadcast_to(
            np.concatenate([hres_b.reshape(-1), hpre_b, hpost_b])[None, :], (P, 24))),
        "gate_in": np.ascontiguousarray(np.broadcast_to(gate.reshape(1, 1), (P, 1))),
        "bv_in": np.ascontiguousarray(bv.reshape(NKV, P).T),
    }
    in_maps = []
    for c in range(8):
        b, chunk = c // 4, c % 4
        m = dict(shared)
        m["x_in"] = np.ascontiguousarray(x[b, chunk * TOK:(chunk + 1) * TOK, :])
        in_maps.append(m)
    return in_maps


def kernel(**inputs):
    nc = _get_prog()
    in_maps = make_in_maps(inputs)
    res = run_bass_kernel_spmd(nc, in_maps, core_ids=list(range(8)))
    out = np.empty((2, S_FULL, D), dtype=np.float32)
    for c in range(8):
        b, chunk = c // 4, c % 4
        out[b, chunk * TOK:(chunk + 1) * TOK, :] = np.asarray(res.results[c]["out"])
    return out


# version 20
# speedup vs baseline: 1.1586x; 1.1586x over previous
import numpy as np
from contextlib import ExitStack

import concourse.bass as bass
import concourse.mybir as mybir
from concourse.bass_utils import run_bass_kernel_spmd

F32 = mybir.dt.float32
BF16 = mybir.dt.bfloat16
AF = mybir.ActivationFunctionType
ALU = mybir.AluOpType
AX = mybir.AxisListType

P = 128
TOK = 1024
TT = 8
D = 4096
NCH = 32
NH = 32
NKV = 8
S_FULL = 4096
KT = 32
LN_EPS = 1e-5
RMS_EPS = 1.1920929e-07
SCALE = float(1.0 / np.sqrt(128.0))
ESHIFT = -25.0
N_ITER = 50


class Sem:
    def __init__(self, h):
        self.h = h
        self.n = 0

    def inc(self, instr, k=1):
        instr.then_inc(self.h, k)
        self.n += k
        return self.n


def build_program(taps=()):
    nc = bass.Bass("TRN2", target_bir_lowering=False)

    x_in = nc.dram_tensor("x_in", [TOK, D], F32, kind="ExternalInput").ap()
    wqT = nc.dram_tensor("wqT", [D, D], F32, kind="ExternalInput").ap()
    wkT = nc.dram_tensor("wkT", [D, 1024], F32, kind="ExternalInput").ap()
    wvT = nc.dram_tensor("wvT", [D, 1024], F32, kind="ExternalInput").ap()
    woT = nc.dram_tensor("woT", [D, D], F32, kind="ExternalInput").ap()
    wbr = nc.dram_tensor("wbr", [1024, 1024], F32, kind="ExternalInput").ap()
    wlogT = nc.dram_tensor("wlogT", [D, 24], F32, kind="ExternalInput").ap()
    bases_in = nc.dram_tensor("bases_in", [P, 24], F32, kind="ExternalInput").ap()
    gate_in = nc.dram_tensor("gate_in", [P, 1], F32, kind="ExternalInput").ap()
    bv_in = nc.dram_tensor("bv_in", [P, NKV], F32, kind="ExternalInput").ap()

    out_d = nc.dram_tensor("out", [TOK, D], F32, kind="ExternalOutput").ap()
    x2_kw = dict(kind="ExternalOutput") if "x2" in taps else {}
    x2_d = nc.dram_tensor("x2", [TOK, D], F32, **x2_kw).ap()
    q_d = nc.dram_tensor("q_sc", [D, TOK], BF16).ap()
    kv_own = nc.dram_tensor("kv_own", [2048, TOK], BF16)
    kv_gath = nc.dram_tensor("kv_gath", [4 * 2048, TOK], BF16)
    tap_d = {}
    if "lnT" in taps:
        tap_d["tap_lnT"] = nc.dram_tensor("tap_lnT", [D, TOK], F32, kind="ExternalOutput").ap()
    if "kv" in taps:
        tap_d["tap_kv"] = nc.dram_tensor("tap_kv", [8192, TOK], F32, kind="ExternalOutput").ap()
    if "q" in taps:
        tap_d["tap_q"] = nc.dram_tensor("tap_q", [D, TOK], F32, kind="ExternalOutput").ap()
    if "ctxT" in taps:
        tap_d["tap_ctxT"] = nc.dram_tensor("tap_ctxT", [D, TOK], F32, kind="ExternalOutput").ap()
    if "ffn" in taps:
        tap_d["tap_ffn"] = nc.dram_tensor("tap_ffn", [P, 96], F32, kind="ExternalOutput").ap()
        tap_d["tap_y"] = nc.dram_tensor("tap_y", [TOK, 1024], F32, kind="ExternalOutput").ap()

    with ExitStack() as ctx:
        en = ctx.enter_context
        ident = en(nc.sbuf_tensor("ident", [P, P], BF16))
        wpan = en(nc.sbuf_tensor("wpan", [P, 2, NCH, P], BF16))
        lnT = en(nc.sbuf_tensor("lnT", [P, NCH, TOK], BF16))
        ctxT = en(nc.sbuf_tensor("ctxT", [P, NCH, TOK], BF16))
        pool_a = en(nc.sbuf_tensor("pool_a", [P, 8448], BF16))
        xt = en(nc.sbuf_tensor("xt", [P, D], F32))
        ln_tok = en(nc.sbuf_tensor("ln_tok", [P, D], BF16))
        qbuf = en(nc.sbuf_tensor("qbuf", [P, TOK], BF16))
        pbuf = en(nc.sbuf_tensor("pbuf", [P, 2, 1024], BF16))
        evac = en(nc.sbuf_tensor("evac", [P, 2, TOK], BF16))
        ctx_st = en(nc.sbuf_tensor("ctx_st", [P, 4, P], BF16))
        wlog_sb = en(nc.sbuf_tensor("wlog_sb", [P, NCH, 24], BF16))
        bases = en(nc.sbuf_tensor("bases", [P, 24], F32))
        gate_sb = en(nc.sbuf_tensor("gate_sb", [P, 1], F32))
        bv_sb = en(nc.sbuf_tensor("bv_sb", [P, NKV], F32))
        bnraw = en(nc.sbuf_tensor("bnraw", [P, 48], F32))
        mv = en(nc.sbuf_tensor("mv", [P, TT, 2], F32))
        rstd = en(nc.sbuf_tensor("rstd", [P, TT], F32))
        nmr = en(nc.sbuf_tensor("nmr", [P, TT], F32))
        rms = en(nc.sbuf_tensor("rms", [P, TT], F32))
        sc1 = en(nc.sbuf_tensor("sc1", [P, TT, 4], F32))
        hps = en(nc.sbuf_tensor("hps", [P, TT, 4], F32))
        rcp = en(nc.sbuf_tensor("rcp", [P, 4], F32))
        Z = en(nc.sbuf_tensor("Z", [P, TT, 16], F32))
        Ebuf = en(nc.sbuf_tensor("Ebuf", [P, TT, 16], F32))
        rsb = en(nc.sbuf_tensor("rsb", [P, TT, 4], F32))
        lsb = en(nc.sbuf_tensor("lsb", [P, TT, 4], F32))
        PL = en(nc.sbuf_tensor("PL", [P, TT, 8], F32))
        Hpre = en(nc.sbuf_tensor("Hpre", [P, TT, 4], F32))
        Hpost = en(nc.sbuf_tensor("Hpost", [P, TT, 4], F32))
        coef = en(nc.sbuf_tensor("coef", [P, TT, 16], F32))
        nb = en(nc.sbuf_tensor("nb", [P, TT, 4], F32))
        gtmp = en(nc.sbuf_tensor("gtmp", [P, 4], F32))
        hsc = en(nc.sbuf_tensor("hsc", [P, TT, 2], F32))
        cb = en(nc.sbuf_tensor("cb", [P, 4], F32))
        Zs = en(nc.sbuf_tensor("Zs", [P, TT, 16], F32))
        mxb = en(nc.sbuf_tensor("mxb", [P, TT, 4], F32))

        ps_s = en(nc.psum_tensor("ps_s", [P, 1024], F32))
        ps_ctx = en(nc.psum_tensor("ps_ctx", [P, 1024], F32))
        ps_tr = en(nc.psum_tensor("ps_tr", [P, 2048], BF16))
        ps_x = en(nc.psum_tensor("ps_x", [P, 1024], F32))

        sW = Sem(en(nc.semaphore("sW")))
        sW0 = Sem(en(nc.semaphore("sW0")))
        sW1 = Sem(en(nc.semaphore("sW1")))
        sX = Sem(en(nc.semaphore("sX")))
        sPE = Sem(en(nc.semaphore("sPE")))
        sV = Sem(en(nc.semaphore("sV")))
        sSC = Sem(en(nc.semaphore("sSC")))
        sG = Sem(en(nc.semaphore("sG")))
        sACC = Sem(en(nc.semaphore("sACC")))
        sPC = Sem(en(nc.semaphore("sPC")))
        sOUT = Sem(en(nc.semaphore("sOUT")))

        block = en(nc.Block())

        K_h = pool_a[:, 0:4096].rearrange("p (r t) -> p r t", r=4)
        V_sb = pool_a[:, 4096:8448].rearrange("p (c f) -> p c f", c=KT)
        v_stage = pool_a[:, 0:8192].rearrange("p (t f) -> p t f", t=TT)
        y_all = v_stage
        x_preT = ctxT[:, 0:8, :]
        xpb = ctxT[:, 12, :]
        xt2 = [xt[:], ctxT[:, 20:28, :].rearrange("p a b -> p (a b)").bitcast(F32)]
        ln2v = [ln_tok[:], ctxT[:, 28:32, :].rearrange("p a b -> p (a b)")]
        aoT = lnT[:, 14:20, :].rearrange("p a (j f) -> p a j f", j=TT)
        ostage = ctxT[:, 16:18, :].rearrange("p a b -> p (a b)").bitcast(F32)
        ostage2 = [ostage, ctxT[:, 18:20, :].rearrange("p a b -> p (a b)").bitcast(F32)]

        @block.sync
        def _(sy):
            sy.dma_start(out=bases[:], in_=bases_in[:, :]).then_inc(sX.h, 16)
            sy.dma_start(out=gate_sb[:], in_=gate_in[:, :]).then_inc(sX.h, 16)
            sy.dma_start(out=bv_sb[:], in_=bv_in[:, :]).then_inc(sX.h, 16)
            sX.n += 48
        tok_consts = sX.n

        @block.scalar
        def _(s):
            s.dma_start(out=x2_d[:, :], in_=x_in[:, :]).then_inc(sPC.h, 16)
            sPC.n += 16
        tok_x2pre = sPC.n

        @block.vector
        def _(v):
            v.wait_ge(sX.h, tok_consts)
            sV.inc(v.tensor_scalar(out=bases[:, 0:16], in0=bases[:, 0:16],
                                   scalar1=10.0, scalar2=None, op0=ALU.mult))
        tok_bases = sV.n

        @block.gpsimd
        def _(g):
            sG.inc(g.memset(ident[:], 0.0))
            g.wait_ge(sG.h, sG.n)
            sG.inc(g.affine_select(out=ident[:], in_=ident[:], compare_op=ALU.not_equal,
                                   fill=1.0, base=0, pattern=[[-1, P]], channel_multiplier=1))
            g.wait_ge(sG.h, sG.n)
            g.memset(cb[:, 0:1], LN_EPS)
            g.memset(cb[:, 1:2], RMS_EPS)
            sG.inc(g.memset(cb[:, 2:3], ESHIFT))
            g.wait_ge(sG.h, sG.n)
            g.dma_start(out=wlog_sb[:], in_=wlogT.rearrange("(c p) m -> p c m", p=P)).then_inc(sW.h, 16)
            sW.n += 16
        tok_ident = sG.n
        tok_wlog = sW.n

        ps_tr_free = [sV.n, sV.n]
        lnT_tile_tok = [0] * TT

        def emit_ln_tile(t, load_tok, want_rms, lnfree=None, xt_ap=None, ln_ap=None):
            xt_ap = xt[:] if xt_ap is None else xt_ap
            ln_ap = ln_tok[:] if ln_ap is None else ln_ap
            @block.vector
            def _(v):
                v.wait_ge(sX.h, load_tok)
                for k in range(8):
                    instr = v.bn_stats(bnraw[:, k * 6:(k + 1) * 6], xt_ap[:, k * 512:(k + 1) * 512])
                    sV.inc(instr)
                v.wait_ge(sV.h, sV.n)
                sV.inc(v.bn_aggr(mv[:, t], bnraw[:]))
            tok_aggr = sV.n

            @block.scalar
            def _(s):
                s.wait_ge(sV.h, tok_aggr)
                s.wait_ge(sG.h, tok_ident)
                sSC.inc(s.activation(out=sc1[:, t, 0:1], in_=mv[:, t, 1:2], func=AF.Ln,
                                     bias=cb[:, 0:1], scale=1.0))
                s.wait_ge(sSC.h, sSC.n)
                sSC.inc(s.activation(out=rstd[:, t:t + 1], in_=sc1[:, t, 0:1], func=AF.Exp,
                                     bias=0.0, scale=-0.5))
            tok_rstd = sSC.n

            @block.vector
            def _(v):
                v.wait_ge(sSC.h, tok_rstd)
                sV.inc(v.tensor_scalar(out=nmr[:, t:t + 1], in0=mv[:, t, 0:1],
                                       scalar1=-1.0, scalar2=None, op0=ALU.mult))
                if want_rms:
                    sV.inc(v.tensor_scalar(out=sc1[:, t, 1:2], in0=mv[:, t, 1:2], scalar1=LN_EPS,
                                           scalar2=None, op0=ALU.add))
                    v.wait_ge(sV.h, sV.n)
                    sV.inc(v.reciprocal(sc1[:, t, 2:3], sc1[:, t, 1:2]))
                    v.wait_ge(sV.h, sV.n)
                    sV.inc(v.tensor_tensor(out=sc1[:, t, 3:4], in0=mv[:, t, 1:2], in1=sc1[:, t, 2:3],
                                           op=ALU.mult))
            tok_nmr = sV.n

            if want_rms:
                @block.scalar
                def _(s):
                    s.wait_ge(sV.h, tok_nmr)
                    sSC.inc(s.activation(out=sc1[:, t, 0:1], in_=sc1[:, t, 3:4], func=AF.Ln,
                                         bias=cb[:, 1:2], scale=1.0))
                    s.wait_ge(sSC.h, sSC.n)
                    sSC.inc(s.activation(out=rms[:, t:t + 1], in_=sc1[:, t, 0:1], func=AF.Exp,
                                         bias=0.0, scale=-0.5))

            @block.vector
            def _(v):
                v.wait_ge(sV.h, tok_nmr)
                if lnfree is not None and lnfree[1]:
                    v.wait_ge(lnfree[0].h, lnfree[1])
                sV.inc(v.tensor_scalar(out=ln_ap, in0=xt_ap,
                                       scalar1=nmr[:, t:t + 1], scalar2=rstd[:, t:t + 1],
                                       op0=ALU.add, op1=ALU.mult))
            tok_lntok = sV.n

            for gi in range(4):
                bank = gi % 2

                @block.tensor
                def _(te):
                    te.wait_ge(sG.h, tok_ident)
                    te.wait_ge(sV.h, tok_lntok)
                    te.wait_ge(sV.h, ps_tr_free[bank])
                    instr = None
                    for j in range(8):
                        c = gi * 8 + j
                        instr = te.transpose(
                            ps_tr[:, bank * 1024 + j * P: bank * 1024 + (j + 1) * P],
                            ln_ap[:, c * P:(c + 1) * P], ident[:])
                    sPE.inc(instr)
                tok_tr = sPE.n

                @block.vector
                def _(v):
                    v.wait_ge(sPE.h, tok_tr)
                    sV.inc(v.tensor_copy(
                        lnT[:, gi * 8:(gi + 1) * 8, t * P:(t + 1) * P],
                        ps_tr[:, bank * 1024:(bank + 1) * 1024].rearrange("p (j f) -> p j f", j=8)))
                ps_tr_free[bank] = sV.n
            lnT_tile_tok[t] = sV.n
            return tok_lntok, sPE.n

        xt_free2 = [0, 0]
        ln_free2 = [0, 0]
        for t in range(TT):
            slot = t % 2

            @block.sync
            def _(sy):
                if xt_free2[slot]:
                    sy.wait_ge(sV.h, xt_free2[slot])
                sy.dma_start(out=xt2[slot], in_=x_in[t * P:(t + 1) * P, :]).then_inc(sX.h, 16)
                sX.n += 16
            tok_load = sX.n
            xt_free2[slot], ln_free2[slot] = emit_ln_tile(
                t, tok_load, want_rms=False, lnfree=(sPE, ln_free2[slot]),
                xt_ap=xt2[slot], ln_ap=ln2v[slot])
        lnT_all = lnT_tile_tok[TT - 1]

        def dump_featmajor(tap_ap, buf, ready_sem, ready_tok):
            for c in range(NCH):
                @block.vector
                def _(v):
                    v.wait_ge(ready_sem.h, ready_tok)
                    v.wait_ge(sX.h, sX.n)
                    sV.inc(v.tensor_copy(ostage, buf[:, c, :]))
                vt = sV.n

                @block.sync
                def _(sy):
                    sy.wait_ge(sV.h, vt)
                    sy.dma_start(out=tap_ap[c * P:(c + 1) * P, :], in_=ostage).then_inc(sX.h, 16)
                    sX.n += 16

        if "lnT" in taps:
            dump_featmajor(tap_d["tap_lnT"], lnT, sV, lnT_all)

        wpan_free = [(sPE, 0), (sPE, 0)]
        ps_free = [(sV, 0), (sV, 0)]

        def emit_panel_load(w_ap, m, nkb, slot):
            fs, ft = wpan_free[slot]
            sw = sW0 if slot == 0 else sW1

            @block.gpsimd
            def _(g):
                if ft:
                    g.wait_ge(fs.h, ft)
                g.dma_start(
                    out=wpan[:, slot, 0:nkb, :],
                    in_=w_ap.rearrange("(kb p) m -> p kb m", p=P)[:, :, m * P:(m + 1) * P],
                ).then_inc(sw.h, 16)
                sw.n += 16
            return (sw, sw.n)

        def emit_proj_mm(slot, nkb, wtok, rhs3d, extra_waits=()):
            ps = ps_s if slot == 0 else ps_x
            fs, ft = ps_free[slot]

            @block.tensor
            def _(te):
                te.wait_ge(wtok[0].h, wtok[1])
                for s_, tv in extra_waits:
                    if tv:
                        te.wait_ge(s_.h, tv)
                if ft:
                    te.wait_ge(fs.h, ft)
                instr = None
                for kb in range(nkb):
                    for half in range(2):
                        instr = te.matmul(ps[:, half * 512:(half + 1) * 512],
                                          wpan[:, slot, kb, :],
                                          rhs3d[:, kb, half * 512:(half + 1) * 512],
                                          start=(kb == 0), stop=(kb == nkb - 1))
                sPE.inc(instr)
            wpan_free[slot] = (sPE, sPE.n)
            return sPE.n, ps

        kv_store_tok = 0
        evac_free = [(sX, 0), (sX, 0)]
        for m in range(NKV):
            slot = m % 2
            wtok = emit_panel_load(wkT, m, NCH, slot)
            mmtok, ps = emit_proj_mm(slot, NCH, wtok, lnT, extra_waits=[(sV, lnT_all)])

            @block.vector
            def _(v):
                v.wait_ge(sPE.h, mmtok)
                efs, eft = evac_free[slot]
                if eft:
                    v.wait_ge(efs.h, eft)
                sV.inc(v.tensor_copy(evac[:, slot], ps[:, :]))
            ps_free[slot] = (sV, sV.n)
            etok = sV.n

            @block.sync
            def _(sy):
                sy.wait_ge(sV.h, etok)
                sy.dma_start(out=kv_own.ap()[m * P:(m + 1) * P, :], in_=evac[:, slot]).then_inc(sX.h, 16)
                sX.n += 16
            evac_free[slot] = (sX, sX.n)
            kv_store_tok = sX.n

        for m in range(NKV):
            slot = m % 2
            wtok = emit_panel_load(wvT, m, NCH, slot)
            mmtok, ps = emit_proj_mm(slot, NCH, wtok, lnT)

            @block.vector
            def _(v):
                v.wait_ge(sPE.h, mmtok)
                efs, eft = evac_free[slot]
                if eft:
                    v.wait_ge(efs.h, eft)
                v.wait_ge(sX.h, tok_consts)
                sV.inc(v.tensor_scalar(out=evac[:, slot], in0=ps[:, :],
                                       scalar1=bv_sb[:, m:m + 1], scalar2=None, op0=ALU.add))
            ps_free[slot] = (sV, sV.n)
            etok = sV.n

            @block.sync
            def _(sy):
                sy.wait_ge(sV.h, etok)
                for tt in range(TT):
                    sy.dma_start_transpose(
                        out=v_stage[:, tt, m * P:(m + 1) * P],
                        in_=evac[:, slot, tt * P:(tt + 1) * P]).then_inc(sX.h, 16)
                    sX.n += 16
            evac_free[slot] = (sX, sX.n)
        vstage_tok = sX.n

        @block.sync
        def _(sy):
            sy.wait_ge(sX.h, vstage_tok)
            sy.dma_start(out=kv_own.ap()[1024:2048, :].rearrange("(t p) f -> p t f", p=P),
                         in_=v_stage[:, :, :]).then_inc(sX.h, 16)
            sX.n += 16
        kv_store_tok = sX.n

        @block.gpsimd
        def _(g):
            g.wait_ge(sX.h, kv_store_tok)
            sG.inc(g.collective_compute(
                "AllGather", ALU.bypass,
                replica_groups=[[0, 1, 2, 3], [4, 5, 6, 7]],
                ins=[kv_own.ap().opt()],
                outs=[kv_gath.ap().opt()],
            ))
        tok_gath = sG.n

        ps_q = ps_tr[:, :].bitcast(F32)
        qstage = [qbuf[:, :], evac[:, 0], evac[:, 1]]
        q_vtok = [0] * NCH
        qstage_free = [(sV, 0), (sV, 0), (sV, 0)]
        ps_q_free = [(sV, 0)]

        def emit_q_chunk(m):
            slot = m % 2
            wtok = emit_panel_load(wqT, m, NCH, slot)

            @block.tensor
            def _(te):
                te.wait_ge(wtok[0].h, wtok[1])
                qfs, qft = ps_q_free[0]
                if qft:
                    te.wait_ge(qfs.h, qft)
                instr = None
                for kb in range(NCH):
                    for hf in range(2):
                        instr = te.matmul(ps_q[:, hf * 512:(hf + 1) * 512],
                                          wpan[:, slot, kb, :],
                                          lnT[:, kb, hf * 512:(hf + 1) * 512],
                                          start=(kb == 0), stop=(kb == NCH - 1))
                sPE.inc(instr)
            wpan_free[slot] = (sPE, sPE.n)
            mmtok = sPE.n

            @block.vector
            def _(v):
                v.wait_ge(sPE.h, mmtok)
                qfs, qft = qstage_free[m % 3]
                if qft:
                    v.wait_ge(qfs.h, qft)
                sV.inc(v.tensor_copy(qstage[m % 3], ps_q[:, :]))
            ps_q_free[0] = (sV, sV.n)
            q_vtok[m] = sV.n

        def dump_dram_bf16(tap_ap, src_ap, nrow, ready_tok_sem, ready_tok):
            for r in range(nrow // P):
                @block.sync
                def _(sy):
                    sy.wait_ge(ready_tok_sem.h, ready_tok)
                    sy.wait_ge(sX.h, sX.n)
                    sy.dma_start(out=evac[:, 0], in_=src_ap[r * P:(r + 1) * P, :]).then_inc(sX.h, 16)
                    sX.n += 16
                lt = sX.n

                @block.vector
                def _(v):
                    v.wait_ge(sX.h, lt)
                    sV.inc(v.tensor_copy(ostage, evac[:, 0]))
                vt = sV.n

                @block.sync
                def _(sy):
                    sy.wait_ge(sV.h, vt)
                    sy.dma_start(out=tap_ap[r * P:(r + 1) * P, :], in_=ostage).then_inc(sX.h, 16)
                    sX.n += 16

        xt_bf = xt[:].bitcast(BF16)
        K_h2 = [K_h, ln_tok[:, 0:4096].rearrange("p (r t) -> p r t", r=4)]
        V_sb2 = [V_sb, xt_bf[:, 0:4352].rearrange("p (c f) -> p c f", c=KT)]
        qbuf2 = [qbuf[:, :], xt_bf[:, 4352:5376]]

        @block.gpsimd
        def _(g):
            g.wait_ge(sX.h, kv_store_tok)
            sG.inc(g.memset(V_sb2[0][:, :, 128:129], 1.0))
            g.wait_ge(sV.h, lnT_all)
            sG.inc(g.memset(V_sb2[1][:, :, 128:129], 1.0))
            g.wait_ge(sG.h, sG.n)
        tok_ones = sG.n

        NPAIR = KT // 2
        kh_busy_slot = [0, 0]
        tok_kvq = [0] * NH

        def emit_head_load(h):
            kv = h // 4
            slot = h % 2

            @block.sync
            def _(sy):
                if kh_busy_slot[slot]:
                    sy.wait_ge(sPE.h, kh_busy_slot[slot])
                if h <= 1:
                    sy.wait_ge(sG.h, tok_gath)
                    sy.wait_ge(sG.h, tok_ones)
                sy.dma_start(
                    out=K_h2[slot][:, :, :],
                    in_=kv_gath.ap().rearrange("(r a) t -> a r t", r=4)[kv * P:(kv + 1) * P],
                ).then_inc(sX.h, 16)
                for r in range(4):
                    sy.dma_start(
                        out=V_sb2[slot][:, r * 8:(r + 1) * 8, 0:128],
                        in_=kv_gath.ap().rearrange("(q s p) f -> q p s f", s=8, p=P)[2 * r + 1][:, :, kv * P:(kv + 1) * P],
                    ).then_inc(sX.h, 16)
                sX.n += 80
            tok_kvq[h] = sX.n

        sc5_base = sSC.n
        order5 = []
        for p_ in range(NPAIR):
            order5.append(("mm", p_))
            if p_ >= 1:
                order5.append(("pv", p_ - 1))
        order5.append(("pv", NPAIR - 1))

        pe5_base = sPE.n

        def half_sched(h, half):
            k = 2 * h + half
            hb = pe5_base + min(h + 3, NCH) + 32 * k
            mm_tok, pv_tok = {}, {}
            ntok = hb
            for kind, p_ in order5:
                ntok += 1
                (mm_tok if kind == "mm" else pv_tok)[p_] = ntok
            sb = sc5_base + k * NPAIR
            exp_tok = {p_: sb + p_ + 1 for p_ in range(NPAIR)}
            return mm_tok, pv_tok, None, exp_tok

        state5 = {"ps_ctx_free": 0, "ctx_st_free": [0, 0, 0, 0]}

        def emit_attn_head(h):
            slot = h % 2
            Kh = K_h2[slot]
            Vh = V_sb2[slot]
            Qh = qstage[h % 3]
            for half in range(2):
                mm_tok, pv_tok, _tr, exp_tok = half_sched(h, half)
                if (h, half) > (0, 0):
                    pexp = half_sched(*((h, 0) if half == 1 else (h - 1, 1)))[3]
                    prev_exp_last2 = (pexp[NPAIR - 2], pexp[NPAIR - 1])
                else:
                    prev_exp_last2 = (0, 0)
                prev_psfree = state5["ps_ctx_free"]

                @block.tensor
                def _(te):
                    te.wait_ge(sX.h, tok_kvq[h])
                    te.wait_ge(sV.h, q_vtok[h])
                    dummies_done = False
                    for kind, p_ in order5:
                        if kind == "mm":
                            if p_ >= 2:
                                te.wait_ge(sSC.h, exp_tok[p_ - 2])
                            elif prev_exp_last2[p_]:
                                te.wait_ge(sSC.h, prev_exp_last2[p_])
                            ps = ps_s if p_ % 2 == 0 else ps_x
                            for cl in range(2):
                                c = 2 * p_ + cl
                                instr = te.matmul(ps[:, cl * 512:(cl + 1) * 512],
                                                  Kh[:, c // 8, (c % 8) * P:(c % 8 + 1) * P],
                                                  Qh[:, half * 512:(half + 1) * 512],
                                                  start=True, stop=True)
                            sPE.inc(instr)
                            assert sPE.n == mm_tok[p_], (h, half, p_, sPE.n, mm_tok[p_])
                        else:
                            if not dummies_done:
                                if prev_psfree:
                                    te.wait_ge(sV.h, prev_psfree)
                                te.matmul(ps_ctx[:, 496:512], ident[:], ident[:, 0:16],
                                          start=True, stop=True)
                                te.matmul(ps_ctx[:, 1008:1024], ident[:], ident[:, 0:16],
                                          start=True, stop=True)
                                dummies_done = True
                            te.wait_ge(sSC.h, exp_tok[p_])
                            for cl in range(2):
                                c = 2 * p_ + cl
                                for qt in range(4):
                                    instr = te.matmul(ps_ctx[:, qt * 256: qt * 256 + 129],
                                                      pbuf[:, p_ % 2, cl * 512 + qt * P: cl * 512 + (qt + 1) * P],
                                                      Vh[:, c, 0:129],
                                                      start=False, stop=(c == KT - 1),
                                                      skip_group_check=True)
                            sPE.inc(instr)
                            assert sPE.n == pv_tok[p_], (h, half, p_, sPE.n, pv_tok[p_])
                kh_busy_slot[slot] = pv_tok[NPAIR - 1]
                qstage_free[h % 3] = (sPE, pv_tok[NPAIR - 1])

                @block.vector
                def _(v):
                    v.wait_ge(sPE.h, pv_tok[NPAIR - 1])
                    if state5["ctx_st_free"][0]:
                        v.wait_ge(sX.h, max(state5["ctx_st_free"]))
                    for qt in range(4):
                        sV.inc(v.reciprocal(rcp[:, qt:qt + 1],
                                            ps_ctx[:, qt * 256 + 128: qt * 256 + 129]))
                    v.wait_ge(sV.h, sV.n)
                    for qt in range(4):
                        sV.inc(v.tensor_scalar(out=ctx_st[:, qt, :],
                                               in0=ps_ctx[:, qt * 256: qt * 256 + 128],
                                               scalar1=rcp[:, qt:qt + 1], scalar2=None,
                                               op0=ALU.mult))
                stok = sV.n
                state5["ps_ctx_free"] = stok

                @block.sync
                def _(sy):
                    sy.wait_ge(sV.h, stok)
                    for qt in range(4):
                        sy.dma_start_transpose(
                            out=ctxT[:, h, half * 512 + qt * P: half * 512 + (qt + 1) * P],
                            in_=ctx_st[:, qt, :]).then_inc(sX.h, 16)
                        sX.n += 16
                for qt in range(4):
                    state5["ctx_st_free"][qt] = sX.n

        for m in range(NCH):
            emit_q_chunk(m)
            if m >= 2:
                emit_attn_head(m - 2)
            emit_head_load(m)
        emit_attn_head(NH - 2)
        emit_attn_head(NH - 1)
        ctx_evac_tok = sX.n

        @block.scalar
        def _(s):
            for h in range(NH):
                for half in range(2):
                    mm_tok, pv_tok, tr_tok, exp_tok = half_sched(h, half)
                    if (h, half) > (0, 0):
                        pmm, ppv, ptr, pexp = half_sched(*((h, 0) if half == 1 else (h - 1, 1)))
                        prev_pv_last2 = (ppv[NPAIR - 2], ppv[NPAIR - 1])
                    else:
                        prev_pv_last2 = (0, 0)
                    for p_ in range(NPAIR):
                        s.wait_ge(sPE.h, mm_tok[p_])
                        if p_ >= 2:
                            s.wait_ge(sPE.h, pv_tok[p_ - 2])
                        elif prev_pv_last2[p_]:
                            s.wait_ge(sPE.h, prev_pv_last2[p_])
                        ps = ps_s if p_ % 2 == 0 else ps_x
                        sSC.inc(s.activation(out=pbuf[:, p_ % 2, :],
                                             in_=ps[:, :],
                                             func=AF.Exp, scale=SCALE, bias=cb[:, 2:3]))
        assert sSC.n == sc5_base + NH * 2 * NPAIR
        ctxT_all = ctx_evac_tok

        if "ctxT" in taps:
            dump_featmajor(tap_d["tap_ctxT"], ctxT, sX, ctxT_all)

        evac_free[0] = qstage_free[1]
        evac_free[1] = qstage_free[2]
        aoT_free = [(sACC, 0)] * 6
        acc_queue = []
        def emit_accum(m, aslot, vt):
            @block.gpsimd
            def _(g):
                g.wait_ge(sV.h, vt)
                g.wait_ge(sPC.h, tok_x2pre)
                g.dma_start(
                    out=x2_d.rearrange("(t p) c -> p t c", p=P)[:, :, m * P:(m + 1) * P],
                    in_=aoT[:, aslot], accum_op=ALU.add,
                ).then_inc(sACC.h, 16)
                sACC.n += 16
            aoT_free[aslot] = (sACC, sACC.n)

        for m in range(NCH):
            slot = m % 2
            wtok = emit_panel_load(woT, m, NCH, slot)
            if len(acc_queue) >= 6:
                emit_accum(*acc_queue.pop(0))
            mmtok, ps = emit_proj_mm(slot, NCH, wtok, ctxT, extra_waits=[(sX, ctxT_all)])

            @block.vector
            def _(v):
                v.wait_ge(sPE.h, mmtok)
                efs, eft = evac_free[slot]
                if eft:
                    v.wait_ge(efs.h, eft)
                sV.inc(v.tensor_copy(evac[:, slot], ps[:, :]))
            ps_free[slot] = (sV, sV.n)
            etok = sV.n

            bank = m % 2
            @block.tensor
            def _(te):
                te.wait_ge(sV.h, etok)
                te.wait_ge(sV.h, ps_tr_free[bank])
                instr = None
                for tt in range(TT):
                    instr = te.transpose(
                        ps_tr[:, bank * 1024 + tt * P: bank * 1024 + (tt + 1) * P],
                        evac[:, slot, tt * P:(tt + 1) * P], ident[:])
                sPE.inc(instr)
            evac_free[slot] = (sPE, sPE.n)
            trtok = sPE.n

            aslot = m % 6
            @block.vector
            def _(v):
                v.wait_ge(sPE.h, trtok)
                afs, aft = aoT_free[aslot]
                if aft:
                    v.wait_ge(afs.h, aft)
                sV.inc(v.tensor_copy(
                    aoT[:, aslot],
                    ps_tr[:, bank * 1024:(bank + 1) * 1024].rearrange("p (j f) -> p j f", j=TT)))
            ps_tr_free[bank] = sV.n
            acc_queue.append((m, aslot, sV.n))
        for item in acc_queue:
            emit_accum(*item)
        x2_done = sACC.n

        logit_psfree = 0
        xpb_free = (sV, 0)
        xpre_done = 0
        xt_free2 = [0, 0]
        ln_free2 = [0, 0]
        for t in range(TT):
            slot = t % 2

            @block.sync
            def _(sy):
                if xt_free2[slot]:
                    sy.wait_ge(sV.h, xt_free2[slot])
                sy.wait_ge(sACC.h, x2_done)
                sy.dma_start(out=xt2[slot], in_=x2_d[t * P:(t + 1) * P, :]).then_inc(sX.h, 16)
                sX.n += 16
            tok_load = sX.n

            xt_sc_tok, lnt_pe_tok = emit_ln_tile(
                t, tok_load, want_rms=True, lnfree=(sPE, ln_free2[slot]),
                xt_ap=xt2[slot], ln_ap=ln2v[slot])
            xt_free2[slot] = xt_sc_tok

            @block.tensor
            def _(te):
                te.wait_ge(sV.h, lnT_tile_tok[t])
                te.wait_ge(sW.h, tok_wlog)
                if logit_psfree:
                    te.wait_ge(sV.h, logit_psfree)
                instr = None
                for c in range(NCH):
                    instr = te.matmul(ps_ctx[:, 0:24],
                                      lnT[:, c, t * P:(t + 1) * P],
                                      wlog_sb[:, c, :],
                                      start=(c == 0), stop=(c == NCH - 1))
                sPE.inc(instr)
            tok_logit = sPE.n

            @block.vector
            def _(v):
                v.wait_ge(sPE.h, tok_logit)
                v.wait_ge(sV.h, tok_bases)
                sV.inc(v.tensor_scalar(out=Z[:, t], in0=ps_ctx[:, 0:16],
                                       scalar1=rms[:, t:t + 1], scalar2=10.0,
                                       op0=ALU.mult, op1=ALU.mult))
                sV.inc(v.tensor_scalar(out=PL[:, t, 0:4], in0=ps_ctx[:, 16:20],
                                       scalar1=rms[:, t:t + 1], scalar2=None, op0=ALU.mult))
                sV.inc(v.tensor_scalar(out=PL[:, t, 4:8], in0=ps_ctx[:, 20:24],
                                       scalar1=rms[:, t:t + 1], scalar2=None, op0=ALU.mult))
                v.wait_ge(sV.h, sV.n)
                sV.inc(v.tensor_tensor(out=Z[:, t], in0=Z[:, t], in1=bases[:, 0:16], op=ALU.add))
                sV.inc(v.tensor_tensor(out=PL[:, t, 0:4], in0=PL[:, t, 0:4],
                                       in1=bases[:, 16:20], op=ALU.add))
                sV.inc(v.tensor_tensor(out=PL[:, t, 4:8], in0=PL[:, t, 4:8],
                                       in1=bases[:, 20:24], op=ALU.add))
            logit_psfree = sV.n

            @block.vector
            def _(v):
                v.wait_ge(sV.h, logit_psfree)
                sV.inc(v.reduce_max(hps[:, t, 0:1], PL[:, t, 0:4], axis=AX.X))
                v.wait_ge(sV.h, sV.n)
                sV.inc(v.tensor_scalar(out=rsb[:, t], in0=PL[:, t, 0:4],
                                       scalar1=hps[:, t, 0:1], scalar2=None, op0=ALU.subtract))
            tok_hp1 = sV.n

            @block.scalar
            def _(s):
                s.wait_ge(sV.h, tok_hp1)
                sSC.inc(s.activation(out=lsb[:, t], in_=rsb[:, t], func=AF.Exp))
            tok_hp2 = sSC.n

            @block.vector
            def _(v):
                v.wait_ge(sSC.h, tok_hp2)
                sV.inc(v.reduce_sum(hps[:, t, 1:2], lsb[:, t], axis=AX.X))
                v.wait_ge(sV.h, sV.n)
                sV.inc(v.reciprocal(hps[:, t, 2:3], hps[:, t, 1:2]))
                v.wait_ge(sV.h, sV.n)
                sV.inc(v.tensor_scalar(out=Hpre[:, t], in0=lsb[:, t],
                                       scalar1=hps[:, t, 2:3], scalar2=None, op0=ALU.mult))
            tok_hpre = sV.n

            @block.vector
            def _(v):
                v.wait_ge(sV.h, tok_hpre)
                v.wait_ge(sV.h, xt_sc_tok)
                fs2, ft2 = xpb_free
                if ft2:
                    v.wait_ge(fs2.h, ft2)
                lnv = ln2v[t % 2]
                sV.inc(v.tensor_scalar(out=ostage, in0=lnv[:, 0:1024],
                                       scalar1=Hpre[:, t, 0:1], scalar2=None, op0=ALU.mult))
                for j in range(1, 4):
                    v.wait_ge(sV.h, sV.n)
                    sV.inc(v.scalar_tensor_tensor(out=ostage, in0=lnv[:, j * 1024:(j + 1) * 1024],
                                                  scalar=Hpre[:, t, j:j + 1], in1=ostage,
                                                  op0=ALU.mult, op1=ALU.add))
                v.wait_ge(sV.h, sV.n)
                sV.inc(v.tensor_copy(xpb, ostage))
            tok_xpb = sV.n

            bank = t % 2
            @block.tensor
            def _(te):
                te.wait_ge(sV.h, tok_xpb)
                te.wait_ge(sV.h, ps_tr_free[bank])
                instr = None
                for cj in range(8):
                    instr = te.transpose(
                        ps_tr[:, bank * 1024 + cj * P: bank * 1024 + (cj + 1) * P],
                        xpb[:, cj * P:(cj + 1) * P], ident[:])
                sPE.inc(instr)
            trtok = sPE.n
            xpb_free = (sPE, trtok)
            ln_free2[t % 2] = max(lnt_pe_tok, trtok)

            @block.vector
            def _(v):
                v.wait_ge(sPE.h, trtok)
                sV.inc(v.tensor_copy(
                    x_preT[:, :, t * P:(t + 1) * P],
                    ps_tr[:, bank * 1024:(bank + 1) * 1024].rearrange("p (j f) -> p j f", j=8)))
            ps_tr_free[bank] = sV.n
        xpre_done = sV.n
        z_done = sV.n

        ybr_done = 0
        for co in range(8):
            slot = co % 2
            wtok = emit_panel_load(wbr, co, 8, slot)
            mmtok, ps = emit_proj_mm(slot, 8, wtok, x_preT, extra_waits=[(sV, xpre_done)])

            @block.vector
            def _(v):
                v.wait_ge(sPE.h, mmtok)
                efs, eft = evac_free[slot]
                if eft:
                    v.wait_ge(efs.h, eft)
                sV.inc(v.tensor_copy(evac[:, slot], ps[:, :]))
            ps_free[slot] = (sV, sV.n)
            etok = sV.n

            bank = co % 2
            @block.tensor
            def _(te):
                te.wait_ge(sV.h, etok)
                te.wait_ge(sV.h, ps_tr_free[bank])
                instr = None
                for tt in range(TT):
                    instr = te.transpose(
                        ps_tr[:, bank * 1024 + tt * P: bank * 1024 + (tt + 1) * P],
                        evac[:, slot, tt * P:(tt + 1) * P], ident[:])
                sPE.inc(instr)
            evac_free[slot] = (sPE, sPE.n)
            trtok = sPE.n

            @block.vector
            def _(v):
                v.wait_ge(sPE.h, trtok)
                if co == 0:
                    v.wait_ge(sPE.h, max(kh_busy_slot))
                sV.inc(v.tensor_copy(
                    y_all[:, :, co * P:(co + 1) * P],
                    ps_tr[:, bank * 1024:(bank + 1) * 1024].rearrange("p (j f) -> p j f", j=TT)))
            ps_tr_free[bank] = sV.n
        ybr_done = sV.n

        sc_base = sSC.n
        v_base = sV.n
        NIT2 = 2 * N_ITER
        vt = lambda k, i: v_base + 10 * k + i
        st = lambda k, i: sc_base + 4 * k + i

        def gviews(g):
            sl = slice(4 * g, 4 * (g + 1))
            return dict(
                Zf=Z[:, sl, :].rearrange("p t e -> p (t e)"),
                Zsf=Zs[:, sl, :].rearrange("p t e -> p (t e)"),
                Ef=Ebuf[:, sl, :].rearrange("p t e -> p (t e)"),
                Z4r=Z[:, sl, :].rearrange("p t (i j) -> p t i j", j=4),
                Z4c=Z[:, sl, :].rearrange("p t (i j) -> p t j i", j=4),
                Zs4r=Zs[:, sl, :].rearrange("p t (i j) -> p t i j", j=4),
                Zs4c=Zs[:, sl, :].rearrange("p t (i j) -> p t j i", j=4),
                E4r=Ebuf[:, sl, :].rearrange("p t (i j) -> p t i j", j=4),
                E4c=Ebuf[:, sl, :].rearrange("p t (i j) -> p t j i", j=4),
                mx=mxb[:, sl, :], rs=rsb[:, sl, :], ls=lsb[:, sl, :],
            )
        GV = [gviews(0), gviews(1)]

        @block.scalar
        def _(s):
            for k in range(NIT2):
                for g in range(2):
                    s.wait_ge(sV.h, vt(k, 2 + 2 * g))
                    sSC.inc(s.activation(out=GV[g]["Ef"], in_=GV[g]["Zsf"], func=AF.Exp))
                    assert sSC.n == st(k, 1 + g)
                for g in range(2):
                    s.wait_ge(sV.h, vt(k, 5 + g))
                    sSC.inc(s.activation(out=GV[g]["ls"], in_=GV[g]["rs"], func=AF.Ln))
                    assert sSC.n == st(k, 3 + g)
            s.wait_ge(sV.h, vt(NIT2 - 1, 8))
            sSC.inc(s.activation(out=GV[0]["Ef"], in_=GV[0]["Zf"], func=AF.Exp))
            s.wait_ge(sV.h, vt(NIT2 - 1, 10))
            sSC.inc(s.activation(out=GV[1]["Ef"], in_=GV[1]["Zf"], func=AF.Exp))
            sSC.inc(s.activation(out=gtmp[:, 0:1], in_=gate_sb[:, 0:1], func=AF.Exp, scale=-1.0))
        assert sSC.n == sc_base + 4 * NIT2 + 3
        tok_hres = sc_base + 4 * NIT2 + 2
        tok_eg = sSC.n

        @block.vector
        def _(v):
            v.wait_ge(sV.h, z_done)
            for k in range(NIT2):
                rows = (k % 2 == 0)
                for g in range(2):
                    Zv = GV[g]["Z4r"] if rows else GV[g]["Z4c"]
                    Zsv = GV[g]["Zs4r"] if rows else GV[g]["Zs4c"]
                    if k > 0:
                        v.wait_ge(sV.h, vt(k - 1, 8 + 2 * g))
                    sV.inc(v.reduce_max(GV[g]["mx"], Zv, axis=AX.X))
                    v.wait_ge(sV.h, sV.n)
                    mB = GV[g]["mx"].unsqueeze(3).broadcast_to([P, 4, 4, 4])
                    sV.inc(v.tensor_tensor(out=Zsv, in0=Zv, in1=mB, op=ALU.subtract))
                    assert sV.n == vt(k, 2 + 2 * g)
                for g in range(2):
                    Ev = GV[g]["E4r"] if rows else GV[g]["E4c"]
                    v.wait_ge(sSC.h, st(k, 1 + g))
                    sV.inc(v.reduce_sum(GV[g]["rs"], Ev, axis=AX.X))
                    assert sV.n == vt(k, 5 + g)
                for g in range(2):
                    Zv = GV[g]["Z4r"] if rows else GV[g]["Z4c"]
                    v.wait_ge(sSC.h, st(k, 3 + g))
                    sV.inc(v.tensor_tensor(out=GV[g]["ls"], in0=GV[g]["ls"],
                                           in1=GV[g]["mx"], op=ALU.add))
                    v.wait_ge(sV.h, sV.n)
                    lsB = GV[g]["ls"].unsqueeze(3).broadcast_to([P, 4, 4, 4])
                    sV.inc(v.tensor_tensor(out=Zv, in0=Zv, in1=lsB, op=ALU.subtract))
                    assert sV.n == vt(k, 8 + 2 * g)
        assert sV.n == v_base + 10 * NIT2

        @block.vector
        def _(v):
            v.wait_ge(sSC.h, tok_eg)
            sV.inc(v.tensor_scalar(out=gtmp[:, 0:1], in0=gtmp[:, 0:1], scalar1=1.0,
                                   scalar2=None, op0=ALU.add))
            v.wait_ge(sV.h, sV.n)
            sV.inc(v.reciprocal(gtmp[:, 1:2], gtmp[:, 0:1]))
            v.wait_ge(sV.h, sV.n)
            sV.inc(v.tensor_scalar(out=gtmp[:, 0:1], in0=gtmp[:, 1:2], scalar1=-1.0,
                                   scalar2=1.0, op0=ALU.mult, op1=ALU.add))
        tok_g = sV.n

        @block.vector
        def _(v):
            v.wait_ge(sV.h, logit_psfree)
            sV.inc(v.reduce_max(hsc[:, :, 0:1], PL[:, :, 4:8], axis=AX.X))
            v.wait_ge(sV.h, sV.n)
            sV.inc(v.tensor_tensor(out=Hpost[:, :, :], in0=PL[:, :, 4:8],
                                   in1=hsc[:, :, 0:1].broadcast_to([P, TT, 4]), op=ALU.subtract))
        tok_po1 = sV.n

        @block.scalar
        def _(s):
            s.wait_ge(sV.h, tok_po1)
            sSC.inc(s.activation(out=Hpost[:, :, :], in_=Hpost[:, :, :], func=AF.Exp))
        tok_po2 = sSC.n

        @block.vector
        def _(v):
            v.wait_ge(sSC.h, tok_po2)
            sV.inc(v.reduce_sum(hsc[:, :, 1:2], Hpost[:, :, :], axis=AX.X))
            v.wait_ge(sV.h, sV.n)
            sV.inc(v.reciprocal(hsc[:, :, 0:1], hsc[:, :, 1:2]))
            v.wait_ge(sV.h, sV.n)
            sV.inc(v.tensor_tensor(out=Hpost[:, :, :], in0=Hpost[:, :, :],
                                   in1=hsc[:, :, 0:1].broadcast_to([P, TT, 4]), op=ALU.mult))
        tok_hpost = sV.n

        @block.vector
        def _(v):
            v.wait_ge(sSC.h, tok_hres)
            v.wait_ge(sV.h, tok_g)
            for t in range(TT):
                sV.inc(v.tensor_tensor(out=gtmp[:, 2:3], in0=gtmp[:, 1:2],
                                       in1=rstd[:, t:t + 1], op=ALU.mult))
                sV.inc(v.tensor_tensor(out=gtmp[:, 3:4], in0=gtmp[:, 0:1],
                                       in1=rstd[:, t:t + 1], op=ALU.mult))
                v.wait_ge(sV.h, sV.n)
                sV.inc(v.tensor_scalar(out=coef[:, t], in0=Ebuf[:, t],
                                       scalar1=gtmp[:, 2:3], scalar2=None, op0=ALU.mult))
                v.wait_ge(sV.h, sV.n)
                for i in range(4):
                    sV.inc(v.tensor_scalar(out=coef[:, t, 5 * i:5 * i + 1],
                                           in0=coef[:, t, 5 * i:5 * i + 1],
                                           scalar1=gtmp[:, 3:4], scalar2=None, op0=ALU.add))
                v.wait_ge(sV.h, sV.n)
                sV.inc(v.reduce_sum(nb[:, t], coef[:, t].rearrange("p (i j) -> p i j", j=4),
                                    axis=AX.X))
                v.wait_ge(sV.h, sV.n)
                sV.inc(v.tensor_scalar(out=nb[:, t], in0=nb[:, t],
                                       scalar1=mv[:, t, 0:1], scalar2=-1.0,
                                       op0=ALU.mult, op1=ALU.mult))
        tok_coef = sV.n

        if "ffn" in taps:
            @block.vector
            def _(v):
                v.wait_ge(sV.h, tok_coef)
                v.wait_ge(sV.h, tok_hpost)
                sV.inc(v.tensor_copy(ostage[:, 0:16], coef[:, 0]))
                sV.inc(v.tensor_copy(ostage[:, 16:20], Hpre[:, 0]))
                sV.inc(v.tensor_copy(ostage[:, 20:24], Hpost[:, 0]))
                sV.inc(v.tensor_copy(ostage[:, 24:28], nb[:, 0]))
                sV.inc(v.tensor_copy(ostage[:, 28:44], Z[:, 0]))
                sV.inc(v.tensor_copy(ostage[:, 44:60], Ebuf[:, 0]))
            vt = sV.n

            @block.sync
            def _(sy):
                sy.wait_ge(sV.h, vt)
                sy.dma_start(out=tap_d["tap_ffn"][:, 0:96], in_=ostage[:, 0:96]).then_inc(sX.h, 16)
                sX.n += 16
            dump_dram_bf16
            for tt_ in range(TT):
                @block.vector
                def _(v):
                    v.wait_ge(sV.h, ybr_done)
                    v.wait_ge(sX.h, sX.n)
                    sV.inc(v.tensor_copy(ostage, y_all[:, tt_, :]))
                vt2 = sV.n

                @block.sync
                def _(sy):
                    sy.wait_ge(sV.h, vt2)
                    sy.dma_start(out=tap_d["tap_y"][tt_ * P:(tt_ + 1) * P, :], in_=ostage).then_inc(sX.h, 16)
                    sX.n += 16

        mix_xt_free = (sV, xt_free2[0])
        for t in range(TT):
            @block.sync
            def _(sy):
                fs, ft = mix_xt_free
                if ft:
                    sy.wait_ge(fs.h, ft)
                sy.dma_start(out=xt[:], in_=x2_d[t * P:(t + 1) * P, :]).then_inc(sX.h, 16)
                sX.n += 16
            tok_load = sX.n

            last_mix_tok = 0
            for i in range(4):
                osl = (t * 4 + i) % 2
                ost = ostage2[osl]
                prev_store = sOUT.n - 16

                @block.vector
                def _(v):
                    v.wait_ge(sX.h, tok_load)
                    v.wait_ge(sV.h, tok_coef)
                    v.wait_ge(sV.h, tok_hpost)
                    v.wait_ge(sV.h, ybr_done)
                    if prev_store > 0:
                        v.wait_ge(sOUT.h, prev_store)
                    sV.inc(v.scalar_tensor_tensor(out=ost, in0=xt[:, 0:1024],
                                                  scalar=coef[:, t, 4 * i:4 * i + 1],
                                                  in1=xt[:, i * 1024:(i + 1) * 1024],
                                                  op0=ALU.mult, op1=ALU.add))
                    for j in range(1, 4):
                        v.wait_ge(sV.h, sV.n)
                        sV.inc(v.scalar_tensor_tensor(out=ost,
                                                      in0=xt[:, j * 1024:(j + 1) * 1024],
                                                      scalar=coef[:, t, 4 * i + j:4 * i + j + 1],
                                                      in1=ost,
                                                      op0=ALU.mult, op1=ALU.add))
                    v.wait_ge(sV.h, sV.n)
                    sV.inc(v.scalar_tensor_tensor(out=ost, in0=y_all[:, t, :],
                                                  scalar=Hpost[:, t, i:i + 1], in1=ost,
                                                  op0=ALU.mult, op1=ALU.add))
                    v.wait_ge(sV.h, sV.n)
                    sV.inc(v.tensor_scalar(out=ost, in0=ost,
                                           scalar1=nb[:, t, i:i + 1], scalar2=None, op0=ALU.add))
                mtok = sV.n
                last_mix_tok = mtok

                @block.sync
                def _(sy):
                    sy.wait_ge(sV.h, mtok)
                    sy.dma_start(out=out_d[t * P:(t + 1) * P, i * 1024:(i + 1) * 1024],
                                 in_=ost).then_inc(sOUT.h, 16)
                    sOUT.n += 16
            mix_xt_free = (sV, last_mix_tok)

        @block.sync
        def _(sy):
            sy.wait_ge(sOUT.h, sOUT.n)

    return nc


_PROG_CACHE = {}


def _get_prog(taps=()):
    key = tuple(sorted(taps))
    if key not in _PROG_CACHE:
        _PROG_CACHE[key] = build_program(key)
    return _PROG_CACHE[key]


def make_in_maps(inputs):
    x = np.asarray(inputs["L0_attn_hidden_states"], dtype=np.float32)
    wq = np.asarray(inputs["L0_attn_Wq"], dtype=np.float32)
    wk = np.asarray(inputs["L0_attn_Wk"], dtype=np.float32)
    wv = np.asarray(inputs["L0_attn_Wv"], dtype=np.float32)
    bv = np.asarray(inputs["L0_attn_bv"], dtype=np.float32)
    wo = np.asarray(inputs["L0_attn_Wo"], dtype=np.float32)
    w_res = np.asarray(inputs["L0_ffn_W_res"], dtype=np.float32)
    w_pre = np.asarray(inputs["L0_ffn_W_pre"], dtype=np.float32)
    w_post = np.asarray(inputs["L0_ffn_W_post"], dtype=np.float32)
    hres_b = np.asarray(inputs["L0_ffn_H_res_base"], dtype=np.float32)
    hpre_b = np.asarray(inputs["L0_ffn_H_pre_base"], dtype=np.float32)
    hpost_b = np.asarray(inputs["L0_ffn_H_post_base"], dtype=np.float32)
    gate = np.asarray(inputs["L0_ffn_gate"], dtype=np.float32)
    wbr_ = np.asarray(inputs["L0_ffn_W_branch"], dtype=np.float32)

    shared = {
        "wqT": np.ascontiguousarray(wq.T),
        "wkT": np.ascontiguousarray(wk.T),
        "wvT": np.ascontiguousarray(wv.T),
        "woT": np.ascontiguousarray(wo.T),
        "wbr": np.ascontiguousarray(wbr_),
        "wlogT": np.ascontiguousarray(np.concatenate([w_res, w_pre, w_post], axis=0).T),
        "bases_in": np.ascontiguousarray(np.broadcast_to(
            np.concatenate([hres_b.reshape(-1), hpre_b, hpost_b])[None, :], (P, 24))),
        "gate_in": np.ascontiguousarray(np.broadcast_to(gate.reshape(1, 1), (P, 1))),
        "bv_in": np.ascontiguousarray(bv.reshape(NKV, P).T),
    }
    in_maps = []
    for c in range(8):
        b, chunk = c // 4, c % 4
        m = dict(shared)
        m["x_in"] = np.ascontiguousarray(x[b, chunk * TOK:(chunk + 1) * TOK, :])
        in_maps.append(m)
    return in_maps


def kernel(**inputs):
    nc = _get_prog()
    in_maps = make_in_maps(inputs)
    res = run_bass_kernel_spmd(nc, in_maps, core_ids=list(range(8)))
    out = np.empty((2, S_FULL, D), dtype=np.float32)
    for c in range(8):
        b, chunk = c // 4, c % 4
        out[b, chunk * TOK:(chunk + 1) * TOK, :] = np.asarray(res.results[c]["out"])
    return out


# version 22
# speedup vs baseline: 1.1731x; 1.0126x over previous
import numpy as np
from contextlib import ExitStack

import concourse.bass as bass
import concourse.mybir as mybir
from concourse.bass_utils import run_bass_kernel_spmd

F32 = mybir.dt.float32
BF16 = mybir.dt.bfloat16
AF = mybir.ActivationFunctionType
ALU = mybir.AluOpType
AX = mybir.AxisListType

P = 128
TOK = 1024
TT = 8
D = 4096
NCH = 32
NH = 32
NKV = 8
S_FULL = 4096
KT = 32
LN_EPS = 1e-5
RMS_EPS = 1.1920929e-07
SCALE = float(1.0 / np.sqrt(128.0))
ESHIFT = -25.0
N_ITER = 50


class Sem:
    def __init__(self, h):
        self.h = h
        self.n = 0

    def inc(self, instr, k=1):
        instr.then_inc(self.h, k)
        self.n += k
        return self.n


def build_program(taps=()):
    nc = bass.Bass("TRN2", target_bir_lowering=False)

    x_in = nc.dram_tensor("x_in", [TOK, D], F32, kind="ExternalInput").ap()
    wqT = nc.dram_tensor("wqT", [D, D], F32, kind="ExternalInput").ap()
    wkT = nc.dram_tensor("wkT", [D, 1024], F32, kind="ExternalInput").ap()
    wvT = nc.dram_tensor("wvT", [D, 1024], F32, kind="ExternalInput").ap()
    woT = nc.dram_tensor("woT", [D, D], F32, kind="ExternalInput").ap()
    wbr = nc.dram_tensor("wbr", [1024, 1024], F32, kind="ExternalInput").ap()
    wlogT = nc.dram_tensor("wlogT", [D, 24], F32, kind="ExternalInput").ap()
    bases_in = nc.dram_tensor("bases_in", [P, 24], F32, kind="ExternalInput").ap()
    gate_in = nc.dram_tensor("gate_in", [P, 1], F32, kind="ExternalInput").ap()
    bv_in = nc.dram_tensor("bv_in", [P, NKV], F32, kind="ExternalInput").ap()

    out_d = nc.dram_tensor("out", [TOK, D], F32, kind="ExternalOutput").ap()
    x2_kw = dict(kind="ExternalOutput") if "x2" in taps else {}
    x2_d = nc.dram_tensor("x2", [TOK, D], F32, **x2_kw).ap()
    q_d = nc.dram_tensor("q_sc", [D, TOK], BF16).ap()
    kv_own = nc.dram_tensor("kv_own", [2048, TOK], BF16)
    kv_gath = nc.dram_tensor("kv_gath", [4 * 2048, TOK], BF16)
    tap_d = {}
    if "lnT" in taps:
        tap_d["tap_lnT"] = nc.dram_tensor("tap_lnT", [D, TOK], F32, kind="ExternalOutput").ap()
    if "kv" in taps:
        tap_d["tap_kv"] = nc.dram_tensor("tap_kv", [8192, TOK], F32, kind="ExternalOutput").ap()
    if "q" in taps:
        tap_d["tap_q"] = nc.dram_tensor("tap_q", [D, TOK], F32, kind="ExternalOutput").ap()
    if "ctxT" in taps:
        tap_d["tap_ctxT"] = nc.dram_tensor("tap_ctxT", [D, TOK], F32, kind="ExternalOutput").ap()
    if "ffn" in taps:
        tap_d["tap_ffn"] = nc.dram_tensor("tap_ffn", [P, 96], F32, kind="ExternalOutput").ap()
        tap_d["tap_y"] = nc.dram_tensor("tap_y", [TOK, 1024], F32, kind="ExternalOutput").ap()

    with ExitStack() as ctx:
        en = ctx.enter_context
        ident = en(nc.sbuf_tensor("ident", [P, P], BF16))
        wpan = en(nc.sbuf_tensor("wpan", [P, 2, NCH, P], BF16))
        lnT = en(nc.sbuf_tensor("lnT", [P, NCH, TOK], BF16))
        ctxT = en(nc.sbuf_tensor("ctxT", [P, NCH, TOK], BF16))
        pool_a = en(nc.sbuf_tensor("pool_a", [P, 8448], BF16))
        xt = en(nc.sbuf_tensor("xt", [P, D], F32))
        ln_tok = en(nc.sbuf_tensor("ln_tok", [P, D], BF16))
        qbuf = en(nc.sbuf_tensor("qbuf", [P, TOK], BF16))
        pbuf = en(nc.sbuf_tensor("pbuf", [P, 2, 1024], BF16))
        evac = en(nc.sbuf_tensor("evac", [P, 2, TOK], BF16))
        ctx_st = en(nc.sbuf_tensor("ctx_st", [P, 4, P], BF16))
        wlog_sb = en(nc.sbuf_tensor("wlog_sb", [P, NCH, 24], BF16))
        bases = en(nc.sbuf_tensor("bases", [P, 24], F32))
        gate_sb = en(nc.sbuf_tensor("gate_sb", [P, 1], F32))
        bv_sb = en(nc.sbuf_tensor("bv_sb", [P, NKV], F32))
        bnraw = en(nc.sbuf_tensor("bnraw", [P, 48], F32))
        mv = en(nc.sbuf_tensor("mv", [P, TT, 2], F32))
        rstd = en(nc.sbuf_tensor("rstd", [P, TT], F32))
        nmr = en(nc.sbuf_tensor("nmr", [P, TT], F32))
        rms = en(nc.sbuf_tensor("rms", [P, TT], F32))
        sc1 = en(nc.sbuf_tensor("sc1", [P, TT, 4], F32))
        hps = en(nc.sbuf_tensor("hps", [P, TT, 4], F32))
        rcp = en(nc.sbuf_tensor("rcp", [P, 4], F32))
        Z = en(nc.sbuf_tensor("Z", [P, TT, 16], F32))
        Ebuf = en(nc.sbuf_tensor("Ebuf", [P, TT, 16], F32))
        rsb = en(nc.sbuf_tensor("rsb", [P, TT, 4], F32))
        lsb = en(nc.sbuf_tensor("lsb", [P, TT, 4], F32))
        PL = en(nc.sbuf_tensor("PL", [P, TT, 8], F32))
        Hpre = en(nc.sbuf_tensor("Hpre", [P, TT, 4], F32))
        Hpost = en(nc.sbuf_tensor("Hpost", [P, TT, 4], F32))
        coef = en(nc.sbuf_tensor("coef", [P, TT, 16], F32))
        nb = en(nc.sbuf_tensor("nb", [P, TT, 4], F32))
        gtmp = en(nc.sbuf_tensor("gtmp", [P, 4], F32))
        hsc = en(nc.sbuf_tensor("hsc", [P, TT, 2], F32))
        cb = en(nc.sbuf_tensor("cb", [P, 4], F32))
        Zs = en(nc.sbuf_tensor("Zs", [P, TT, 16], F32))
        mxb = en(nc.sbuf_tensor("mxb", [P, TT, 4], F32))

        ps_s = en(nc.psum_tensor("ps_s", [P, 1024], F32))
        ps_ctx = en(nc.psum_tensor("ps_ctx", [P, 1024], F32))
        ps_tr = en(nc.psum_tensor("ps_tr", [P, 2048], BF16))
        ps_x = en(nc.psum_tensor("ps_x", [P, 1024], F32))

        sW = Sem(en(nc.semaphore("sW")))
        sW0 = Sem(en(nc.semaphore("sW0")))
        sW1 = Sem(en(nc.semaphore("sW1")))
        sX = Sem(en(nc.semaphore("sX")))
        sPE = Sem(en(nc.semaphore("sPE")))
        sV = Sem(en(nc.semaphore("sV")))
        sSC = Sem(en(nc.semaphore("sSC")))
        sG = Sem(en(nc.semaphore("sG")))
        sACC = Sem(en(nc.semaphore("sACC")))
        sPC = Sem(en(nc.semaphore("sPC")))
        sCN = Sem(en(nc.semaphore("sCN")))
        sXA = Sem(en(nc.semaphore("sXA")))
        sKV = [Sem(en(nc.semaphore("sKV0"))), Sem(en(nc.semaphore("sKV1")))]
        sH = [Sem(en(nc.semaphore("sH0"))), Sem(en(nc.semaphore("sH1")))]
        sCT = Sem(en(nc.semaphore("sCT")))
        sOUT = Sem(en(nc.semaphore("sOUT")))

        block = en(nc.Block())

        K_h = pool_a[:, 0:4096].rearrange("p (r t) -> p r t", r=4)
        V_sb = pool_a[:, 4096:8448].rearrange("p (c f) -> p c f", c=KT)
        v_stage = pool_a[:, 0:8192].rearrange("p (t f) -> p t f", t=TT)
        y_all = v_stage
        x_preT = ctxT[:, 0:8, :]
        xpb = ctxT[:, 12, :]
        xt2 = [xt[:], ctxT[:, 20:28, :].rearrange("p a b -> p (a b)").bitcast(F32)]
        ln2v = [ln_tok[:], ctxT[:, 28:32, :].rearrange("p a b -> p (a b)")]
        aoT = lnT[:, 14:20, :].rearrange("p a (j f) -> p a j f", j=TT)
        ostage = ctxT[:, 16:18, :].rearrange("p a b -> p (a b)").bitcast(F32)
        ostage2 = [ostage, ctxT[:, 18:20, :].rearrange("p a b -> p (a b)").bitcast(F32)]

        @block.sync
        def _(sy):
            sy.dma_start(out=bases[:], in_=bases_in[:, :]).then_inc(sCN.h, 16)
            sy.dma_start(out=gate_sb[:], in_=gate_in[:, :]).then_inc(sCN.h, 16)
            sy.dma_start(out=bv_sb[:], in_=bv_in[:, :]).then_inc(sCN.h, 16)
            sCN.n += 48
        tok_consts = sCN.n

        @block.scalar
        def _(s):
            s.dma_start(out=x2_d[:, :], in_=x_in[:, :]).then_inc(sPC.h, 16)
            sPC.n += 16
        tok_x2pre = sPC.n

        @block.vector
        def _(v):
            v.wait_ge(sCN.h, tok_consts)
            sV.inc(v.tensor_scalar(out=bases[:, 0:16], in0=bases[:, 0:16],
                                   scalar1=10.0, scalar2=None, op0=ALU.mult))
        tok_bases = sV.n

        @block.gpsimd
        def _(g):
            sG.inc(g.memset(ident[:], 0.0))
            g.wait_ge(sG.h, sG.n)
            sG.inc(g.affine_select(out=ident[:], in_=ident[:], compare_op=ALU.not_equal,
                                   fill=1.0, base=0, pattern=[[-1, P]], channel_multiplier=1))
            g.wait_ge(sG.h, sG.n)
            g.memset(cb[:, 0:1], LN_EPS)
            g.memset(cb[:, 1:2], RMS_EPS)
            sG.inc(g.memset(cb[:, 2:3], ESHIFT))
            g.wait_ge(sG.h, sG.n)
            g.dma_start(out=wlog_sb[:], in_=wlogT.rearrange("(c p) m -> p c m", p=P)).then_inc(sW.h, 16)
            sW.n += 16
        tok_ident = sG.n
        tok_wlog = sW.n

        ps_tr_free = [sV.n, sV.n]
        lnT_tile_tok = [0] * TT

        def emit_ln_tile(t, load_tok, want_rms, lnfree=None, xt_ap=None, ln_ap=None):
            xt_ap = xt[:] if xt_ap is None else xt_ap
            ln_ap = ln_tok[:] if ln_ap is None else ln_ap
            @block.vector
            def _(v):
                v.wait_ge(sXA.h, load_tok)
                for k in range(8):
                    instr = v.bn_stats(bnraw[:, k * 6:(k + 1) * 6], xt_ap[:, k * 512:(k + 1) * 512])
                    sV.inc(instr)
                v.wait_ge(sV.h, sV.n)
                sV.inc(v.bn_aggr(mv[:, t], bnraw[:]))
            tok_aggr = sV.n

            @block.scalar
            def _(s):
                s.wait_ge(sV.h, tok_aggr)
                s.wait_ge(sG.h, tok_ident)
                sSC.inc(s.activation(out=sc1[:, t, 0:1], in_=mv[:, t, 1:2], func=AF.Ln,
                                     bias=cb[:, 0:1], scale=1.0))
                s.wait_ge(sSC.h, sSC.n)
                sSC.inc(s.activation(out=rstd[:, t:t + 1], in_=sc1[:, t, 0:1], func=AF.Exp,
                                     bias=0.0, scale=-0.5))
            tok_rstd = sSC.n

            @block.vector
            def _(v):
                v.wait_ge(sSC.h, tok_rstd)
                sV.inc(v.tensor_scalar(out=nmr[:, t:t + 1], in0=mv[:, t, 0:1],
                                       scalar1=-1.0, scalar2=None, op0=ALU.mult))
                if want_rms:
                    sV.inc(v.tensor_scalar(out=sc1[:, t, 1:2], in0=mv[:, t, 1:2], scalar1=LN_EPS,
                                           scalar2=None, op0=ALU.add))
                    v.wait_ge(sV.h, sV.n)
                    sV.inc(v.reciprocal(sc1[:, t, 2:3], sc1[:, t, 1:2]))
                    v.wait_ge(sV.h, sV.n)
                    sV.inc(v.tensor_tensor(out=sc1[:, t, 3:4], in0=mv[:, t, 1:2], in1=sc1[:, t, 2:3],
                                           op=ALU.mult))
            tok_nmr = sV.n

            if want_rms:
                @block.scalar
                def _(s):
                    s.wait_ge(sV.h, tok_nmr)
                    sSC.inc(s.activation(out=sc1[:, t, 0:1], in_=sc1[:, t, 3:4], func=AF.Ln,
                                         bias=cb[:, 1:2], scale=1.0))
                    s.wait_ge(sSC.h, sSC.n)
                    sSC.inc(s.activation(out=rms[:, t:t + 1], in_=sc1[:, t, 0:1], func=AF.Exp,
                                         bias=0.0, scale=-0.5))

            @block.vector
            def _(v):
                v.wait_ge(sV.h, tok_nmr)
                if lnfree is not None and lnfree[1]:
                    v.wait_ge(lnfree[0].h, lnfree[1])
                sV.inc(v.tensor_scalar(out=ln_ap, in0=xt_ap,
                                       scalar1=nmr[:, t:t + 1], scalar2=rstd[:, t:t + 1],
                                       op0=ALU.add, op1=ALU.mult))
            tok_lntok = sV.n

            for gi in range(4):
                bank = gi % 2

                @block.tensor
                def _(te):
                    te.wait_ge(sG.h, tok_ident)
                    te.wait_ge(sV.h, tok_lntok)
                    te.wait_ge(sV.h, ps_tr_free[bank])
                    instr = None
                    for j in range(8):
                        c = gi * 8 + j
                        instr = te.transpose(
                            ps_tr[:, bank * 1024 + j * P: bank * 1024 + (j + 1) * P],
                            ln_ap[:, c * P:(c + 1) * P], ident[:])
                    sPE.inc(instr)
                tok_tr = sPE.n

                @block.vector
                def _(v):
                    v.wait_ge(sPE.h, tok_tr)
                    sV.inc(v.tensor_copy(
                        lnT[:, gi * 8:(gi + 1) * 8, t * P:(t + 1) * P],
                        ps_tr[:, bank * 1024:(bank + 1) * 1024].rearrange("p (j f) -> p j f", j=8)))
                ps_tr_free[bank] = sV.n
            lnT_tile_tok[t] = sV.n
            return tok_lntok, sPE.n

        xt_free2 = [0, 0]
        ln_free2 = [0, 0]
        for t in range(TT):
            slot = t % 2

            @block.sync
            def _(sy):
                if xt_free2[slot]:
                    sy.wait_ge(sV.h, xt_free2[slot])
                sy.dma_start(out=xt2[slot], in_=x_in[t * P:(t + 1) * P, :]).then_inc(sXA.h, 16)
                sXA.n += 16
            tok_load = sXA.n
            xt_free2[slot], ln_free2[slot] = emit_ln_tile(
                t, tok_load, want_rms=False, lnfree=(sPE, ln_free2[slot]),
                xt_ap=xt2[slot], ln_ap=ln2v[slot])
        lnT_all = lnT_tile_tok[TT - 1]

        def dump_featmajor(tap_ap, buf, ready_sem, ready_tok):
            for c in range(NCH):
                @block.vector
                def _(v):
                    v.wait_ge(ready_sem.h, ready_tok)
                    v.wait_ge(sX.h, sX.n)
                    sV.inc(v.tensor_copy(ostage, buf[:, c, :]))
                vt = sV.n

                @block.sync
                def _(sy):
                    sy.wait_ge(sV.h, vt)
                    sy.dma_start(out=tap_ap[c * P:(c + 1) * P, :], in_=ostage).then_inc(sX.h, 16)
                    sX.n += 16

        if "lnT" in taps:
            dump_featmajor(tap_d["tap_lnT"], lnT, sV, lnT_all)

        wpan_free = [(sPE, 0), (sPE, 0)]
        ps_free = [(sV, 0), (sV, 0)]

        def emit_panel_load(w_ap, m, nkb, slot):
            fs, ft = wpan_free[slot]
            sw = sW0 if slot == 0 else sW1

            @block.gpsimd
            def _(g):
                if ft:
                    g.wait_ge(fs.h, ft)
                g.dma_start(
                    out=wpan[:, slot, 0:nkb, :],
                    in_=w_ap.rearrange("(kb p) m -> p kb m", p=P)[:, :, m * P:(m + 1) * P],
                ).then_inc(sw.h, 16)
                sw.n += 16
            return (sw, sw.n)

        def emit_proj_mm(slot, nkb, wtok, rhs3d, extra_waits=()):
            ps = ps_s if slot == 0 else ps_x
            fs, ft = ps_free[slot]

            @block.tensor
            def _(te):
                te.wait_ge(wtok[0].h, wtok[1])
                for s_, tv in extra_waits:
                    if tv:
                        te.wait_ge(s_.h, tv)
                if ft:
                    te.wait_ge(fs.h, ft)
                instr = None
                for kb in range(nkb):
                    for half in range(2):
                        instr = te.matmul(ps[:, half * 512:(half + 1) * 512],
                                          wpan[:, slot, kb, :],
                                          rhs3d[:, kb, half * 512:(half + 1) * 512],
                                          start=(kb == 0), stop=(kb == nkb - 1))
                sPE.inc(instr)
            wpan_free[slot] = (sPE, sPE.n)
            return sPE.n, ps

        kv_store_tok = 0
        evac_free = [(sX, 0), (sX, 0)]
        for m in range(NKV):
            slot = m % 2
            wtok = emit_panel_load(wkT, m, NCH, slot)
            mmtok, ps = emit_proj_mm(slot, NCH, wtok, lnT, extra_waits=[(sV, lnT_all)])

            @block.vector
            def _(v):
                v.wait_ge(sPE.h, mmtok)
                efs, eft = evac_free[slot]
                if eft:
                    v.wait_ge(efs.h, eft)
                sV.inc(v.tensor_copy(evac[:, slot], ps[:, :]))
            ps_free[slot] = (sV, sV.n)
            etok = sV.n

            @block.sync
            def _(sy):
                sy.wait_ge(sV.h, etok)
                sy.dma_start(out=kv_own.ap()[m * P:(m + 1) * P, :], in_=evac[:, slot]).then_inc(sKV[slot].h, 16)
                sKV[slot].n += 16
            evac_free[slot] = (sKV[slot], sKV[slot].n)

        for m in range(NKV):
            slot = m % 2
            wtok = emit_panel_load(wvT, m, NCH, slot)
            mmtok, ps = emit_proj_mm(slot, NCH, wtok, lnT)

            @block.vector
            def _(v):
                v.wait_ge(sPE.h, mmtok)
                efs, eft = evac_free[slot]
                if eft:
                    v.wait_ge(efs.h, eft)
                v.wait_ge(sCN.h, tok_consts)
                sV.inc(v.tensor_scalar(out=evac[:, slot], in0=ps[:, :],
                                       scalar1=bv_sb[:, m:m + 1], scalar2=None, op0=ALU.add))
            ps_free[slot] = (sV, sV.n)
            etok = sV.n

            @block.sync
            def _(sy):
                sy.wait_ge(sV.h, etok)
                for tt in range(TT):
                    sy.dma_start_transpose(
                        out=v_stage[:, tt, m * P:(m + 1) * P],
                        in_=evac[:, slot, tt * P:(tt + 1) * P]).then_inc(sKV[slot].h, 16)
                    sKV[slot].n += 16
            evac_free[slot] = (sKV[slot], sKV[slot].n)
        vstage_toks = (sKV[0].n, sKV[1].n)

        @block.sync
        def _(sy):
            sy.wait_ge(sKV[0].h, vstage_toks[0])
            sy.wait_ge(sKV[1].h, vstage_toks[1])
            sy.dma_start(out=kv_own.ap()[1024:2048, :].rearrange("(t p) f -> p t f", p=P),
                         in_=v_stage[:, :, :]).then_inc(sKV[0].h, 16)
            sKV[0].n += 16
        kv_store_toks = (sKV[0].n, sKV[1].n)

        @block.gpsimd
        def _(g):
            g.wait_ge(sKV[0].h, kv_store_toks[0])
            g.wait_ge(sKV[1].h, kv_store_toks[1])
            sG.inc(g.collective_compute(
                "AllGather", ALU.bypass,
                replica_groups=[[0, 1, 2, 3], [4, 5, 6, 7]],
                ins=[kv_own.ap().opt()],
                outs=[kv_gath.ap().opt()],
            ))
        tok_gath = sG.n

        q_store_tok = 0
        for m in range(NCH):
            slot = m % 2
            wtok = emit_panel_load(wqT, m, NCH, slot)
            mmtok, ps = emit_proj_mm(slot, NCH, wtok, lnT)

            @block.vector
            def _(v):
                v.wait_ge(sPE.h, mmtok)
                efs, eft = evac_free[slot]
                if eft:
                    v.wait_ge(efs.h, eft)
                sV.inc(v.tensor_copy(evac[:, slot], ps[:, :]))
            ps_free[slot] = (sV, sV.n)
            etok = sV.n

            @block.sync
            def _(sy):
                sy.wait_ge(sV.h, etok)
                sy.dma_start(out=q_d[m * P:(m + 1) * P, :], in_=evac[:, slot]).then_inc(sKV[slot].h, 16)
                sKV[slot].n += 16
            evac_free[slot] = (sKV[slot], sKV[slot].n)
            q_store_toks = (sKV[0].n, sKV[1].n)

        def dump_dram_bf16(tap_ap, src_ap, nrow, ready_tok_sem, ready_tok):
            for r in range(nrow // P):
                @block.sync
                def _(sy):
                    sy.wait_ge(ready_tok_sem.h, ready_tok)
                    sy.wait_ge(sX.h, sX.n)
                    sy.dma_start(out=evac[:, 0], in_=src_ap[r * P:(r + 1) * P, :]).then_inc(sX.h, 16)
                    sX.n += 16
                lt = sX.n

                @block.vector
                def _(v):
                    v.wait_ge(sX.h, lt)
                    sV.inc(v.tensor_copy(ostage, evac[:, 0]))
                vt = sV.n

                @block.sync
                def _(sy):
                    sy.wait_ge(sV.h, vt)
                    sy.dma_start(out=tap_ap[r * P:(r + 1) * P, :], in_=ostage).then_inc(sX.h, 16)
                    sX.n += 16

        if "kv" in taps:
            dump_dram_bf16(tap_d["tap_kv"], kv_gath.ap(), 8192, sG, tok_gath)
        if "q" in taps:
            dump_dram_bf16(tap_d["tap_q"], q_d, D, sX, q_store_tok)

        lnT_flat = lnT[:, :, :].rearrange("p c t -> p (c t)")
        K_h2 = [K_h, lnT_flat[:, 0:4096].rearrange("p (r t) -> p r t", r=4)]
        V_sb2 = [V_sb, lnT_flat[:, 4096:8448].rearrange("p (c f) -> p c f", c=KT)]
        qbuf2 = [qbuf[:, :], lnT_flat[:, 8448:9472]]
        q_proj_done = wpan_free[1][1] if wpan_free[1][0] is sPE else sPE.n
        q_proj_done = sPE.n

        @block.gpsimd
        def _(g):
            g.wait_ge(sKV[0].h, kv_store_toks[0])
            g.wait_ge(sKV[1].h, kv_store_toks[1])
            sG.inc(g.memset(V_sb2[0][:, :, 128:129], 1.0))
            g.wait_ge(sPE.h, q_proj_done)
            sG.inc(g.memset(V_sb2[1][:, :, 128:129], 1.0))
            g.wait_ge(sG.h, sG.n)
        tok_ones = sG.n

        NPAIR = KT // 2
        kh_busy_slot = [0, 0]
        ctx_evac_tok = 0
        ctx_st_free = [0, 0]
        tok_kvq = [0] * NH

        def emit_head_load(h):
            kv = h // 4
            slot = h % 2

            @block.sync
            def _(sy):
                if kh_busy_slot[slot]:
                    sy.wait_ge(sPE.h, kh_busy_slot[slot])
                if h == 0:
                    sy.wait_ge(sG.h, tok_gath)
                    sy.wait_ge(sG.h, tok_ones)
                    sy.wait_ge(sKV[0].h, q_store_toks[0])
                    sy.wait_ge(sKV[1].h, q_store_toks[1])
                if h == 1:
                    sy.wait_ge(sG.h, tok_ones)
                sy.dma_start(
                    out=K_h2[slot][:, :, :],
                    in_=kv_gath.ap().rearrange("(r a) t -> a r t", r=4)[kv * P:(kv + 1) * P],
                ).then_inc(sH[slot].h, 16)
                for r in range(4):
                    sy.dma_start(
                        out=V_sb2[slot][:, r * 8:(r + 1) * 8, 0:128],
                        in_=kv_gath.ap().rearrange("(q s p) f -> q p s f", s=8, p=P)[2 * r + 1][:, :, kv * P:(kv + 1) * P],
                    ).then_inc(sH[slot].h, 16)
                sy.dma_start(out=qbuf2[slot], in_=q_d[h * P:(h + 1) * P, :]).then_inc(sH[slot].h, 16)
                sH[slot].n += 96
            tok_kvq[h] = sH[slot].n

        emit_head_load(0)
        emit_head_load(1)

        pe5_base = sPE.n
        sc5_base = sSC.n
        order5 = []
        for p_ in range(NPAIR):
            order5.append(("mm", p_))
            if p_ >= 1:
                order5.append(("pv", p_ - 1))
        order5.append(("pv", NPAIR - 1))

        def half_sched(h, half):
            k = 2 * h + half
            hb = pe5_base + 32 * k
            mm_tok, pv_tok = {}, {}
            ntok = hb
            for kind, p_ in order5:
                ntok += 1
                (mm_tok if kind == "mm" else pv_tok)[p_] = ntok
            tr_tok = None
            sb = sc5_base + k * NPAIR
            exp_tok = {p_: sb + p_ + 1 for p_ in range(NPAIR)}
            return mm_tok, pv_tok, tr_tok, exp_tok

        ps_ctx_free = 0
        ctx_st_free = [0, 0, 0, 0]

        for h in range(NH):
            slot = h % 2
            Kh = K_h2[slot]
            Vh = V_sb2[slot]
            Qh = qbuf2[slot]

            for half in range(2):
                mm_tok, pv_tok, tr_tok, exp_tok = half_sched(h, half)
                if (h, half) > (0, 0):
                    pexp = half_sched(*((h, 0) if half == 1 else (h - 1, 1)))[3]
                    prev_exp_last2 = (pexp[NPAIR - 2], pexp[NPAIR - 1])
                else:
                    prev_exp_last2 = (0, 0)
                prev_psfree = ps_ctx_free

                @block.tensor
                def _(te):
                    te.wait_ge(sH[h % 2].h, tok_kvq[h])
                    dummies_done = False
                    for kind, p_ in order5:
                        if kind == "mm":
                            if p_ >= 2:
                                te.wait_ge(sSC.h, exp_tok[p_ - 2])
                            elif prev_exp_last2[p_]:
                                te.wait_ge(sSC.h, prev_exp_last2[p_])
                            ps = ps_s if p_ % 2 == 0 else ps_x
                            for cl in range(2):
                                c = 2 * p_ + cl
                                instr = te.matmul(ps[:, cl * 512:(cl + 1) * 512],
                                                  Kh[:, c // 8, (c % 8) * P:(c % 8 + 1) * P],
                                                  Qh[:, half * 512:(half + 1) * 512],
                                                  start=True, stop=True)
                            sPE.inc(instr)
                            assert sPE.n == mm_tok[p_]
                        else:
                            if not dummies_done:
                                if prev_psfree:
                                    te.wait_ge(sV.h, prev_psfree)
                                te.matmul(ps_ctx[:, 496:512], ident[:], ident[:, 0:16],
                                          start=True, stop=True)
                                te.matmul(ps_ctx[:, 1008:1024], ident[:], ident[:, 0:16],
                                          start=True, stop=True)
                                dummies_done = True
                            te.wait_ge(sSC.h, exp_tok[p_])
                            for cl in range(2):
                                c = 2 * p_ + cl
                                for qt in range(4):
                                    instr = te.matmul(ps_ctx[:, qt * 256: qt * 256 + 129],
                                                      pbuf[:, p_ % 2, cl * 512 + qt * P: cl * 512 + (qt + 1) * P],
                                                      Vh[:, c, 0:129],
                                                      start=False, stop=(c == KT - 1),
                                                      skip_group_check=True)
                            sPE.inc(instr)
                            assert sPE.n == pv_tok[p_]
                kh_busy_slot[slot] = pv_tok[NPAIR - 1]

                @block.vector
                def _(v):
                    v.wait_ge(sPE.h, pv_tok[NPAIR - 1])
                    if ctx_st_free[0]:
                        v.wait_ge(sCT.h, max(ctx_st_free))
                    for qt in range(4):
                        sV.inc(v.reciprocal(rcp[:, qt:qt + 1],
                                            ps_ctx[:, qt * 256 + 128: qt * 256 + 129]))
                    v.wait_ge(sV.h, sV.n)
                    for qt in range(4):
                        sV.inc(v.tensor_scalar(out=ctx_st[:, qt, :],
                                               in0=ps_ctx[:, qt * 256: qt * 256 + 128],
                                               scalar1=rcp[:, qt:qt + 1], scalar2=None,
                                               op0=ALU.mult))
                stok = sV.n
                ps_ctx_free = stok

                @block.sync
                def _(sy):
                    sy.wait_ge(sV.h, stok)
                    for qt in range(4):
                        sy.dma_start_transpose(
                            out=ctxT[:, h, half * 512 + qt * P: half * 512 + (qt + 1) * P],
                            in_=ctx_st[:, qt, :]).then_inc(sCT.h, 16)
                        sCT.n += 16
                for qt in range(4):
                    ctx_st_free[qt] = sCT.n
            if h + 2 < NH:
                emit_head_load(h + 2)
        ctx_evac_tok = sCT.n

        @block.scalar
        def _(s):
            for h in range(NH):
                for half in range(2):
                    mm_tok, pv_tok, tr_tok, exp_tok = half_sched(h, half)
                    if (h, half) > (0, 0):
                        pmm, ppv, ptr, pexp = half_sched(*((h, 0) if half == 1 else (h - 1, 1)))
                        prev_pv_last2 = (ppv[NPAIR - 2], ppv[NPAIR - 1])
                    else:
                        prev_pv_last2 = (0, 0)
                    for p_ in range(NPAIR):
                        s.wait_ge(sPE.h, mm_tok[p_])
                        if p_ >= 2:
                            s.wait_ge(sPE.h, pv_tok[p_ - 2])
                        elif prev_pv_last2[p_]:
                            s.wait_ge(sPE.h, prev_pv_last2[p_])
                        ps = ps_s if p_ % 2 == 0 else ps_x
                        sSC.inc(s.activation(out=pbuf[:, p_ % 2, :],
                                             in_=ps[:, :],
                                             func=AF.Exp, scale=SCALE, bias=cb[:, 2:3]))
        assert sSC.n == sc5_base + NH * 2 * NPAIR
        ctxT_all = ctx_evac_tok

        if "ctxT" in taps:
            dump_featmajor(tap_d["tap_ctxT"], ctxT, sX, ctxT_all)

        aoT_free = [(sACC, 0)] * 6
        acc_queue = []
        def emit_accum(m, aslot, vt):
            @block.gpsimd
            def _(g):
                g.wait_ge(sV.h, vt)
                g.wait_ge(sPC.h, tok_x2pre)
                g.dma_start(
                    out=x2_d.rearrange("(t p) c -> p t c", p=P)[:, :, m * P:(m + 1) * P],
                    in_=aoT[:, aslot], accum_op=ALU.add,
                ).then_inc(sACC.h, 16)
                sACC.n += 16
            aoT_free[aslot] = (sACC, sACC.n)

        for m in range(NCH):
            slot = m % 2
            wtok = emit_panel_load(woT, m, NCH, slot)
            if len(acc_queue) >= 6:
                emit_accum(*acc_queue.pop(0))
            mmtok, ps = emit_proj_mm(slot, NCH, wtok, ctxT, extra_waits=[(sCT, ctxT_all)])

            @block.vector
            def _(v):
                v.wait_ge(sPE.h, mmtok)
                efs, eft = evac_free[slot]
                if eft:
                    v.wait_ge(efs.h, eft)
                sV.inc(v.tensor_copy(evac[:, slot], ps[:, :]))
            ps_free[slot] = (sV, sV.n)
            etok = sV.n

            bank = m % 2
            @block.tensor
            def _(te):
                te.wait_ge(sV.h, etok)
                te.wait_ge(sV.h, ps_tr_free[bank])
                instr = None
                for tt in range(TT):
                    instr = te.transpose(
                        ps_tr[:, bank * 1024 + tt * P: bank * 1024 + (tt + 1) * P],
                        evac[:, slot, tt * P:(tt + 1) * P], ident[:])
                sPE.inc(instr)
            evac_free[slot] = (sPE, sPE.n)
            trtok = sPE.n

            aslot = m % 6
            @block.vector
            def _(v):
                v.wait_ge(sPE.h, trtok)
                afs, aft = aoT_free[aslot]
                if aft:
                    v.wait_ge(afs.h, aft)
                sV.inc(v.tensor_copy(
                    aoT[:, aslot],
                    ps_tr[:, bank * 1024:(bank + 1) * 1024].rearrange("p (j f) -> p j f", j=TT)))
            ps_tr_free[bank] = sV.n
            acc_queue.append((m, aslot, sV.n))
        for item in acc_queue:
            emit_accum(*item)
        x2_done = sACC.n

        logit_psfree = 0
        xpb_free = (sV, 0)
        xpre_done = 0
        xt_free2 = [0, 0]
        ln_free2 = [0, 0]
        for t in range(TT):
            slot = t % 2

            @block.sync
            def _(sy):
                if xt_free2[slot]:
                    sy.wait_ge(sV.h, xt_free2[slot])
                sy.wait_ge(sACC.h, x2_done)
                sy.dma_start(out=xt2[slot], in_=x2_d[t * P:(t + 1) * P, :]).then_inc(sXA.h, 16)
                sXA.n += 16
            tok_load = sXA.n

            xt_sc_tok, lnt_pe_tok = emit_ln_tile(
                t, tok_load, want_rms=True, lnfree=(sPE, ln_free2[slot]),
                xt_ap=xt2[slot], ln_ap=ln2v[slot])
            xt_free2[slot] = xt_sc_tok

            @block.tensor
            def _(te):
                te.wait_ge(sV.h, lnT_tile_tok[t])
                te.wait_ge(sW.h, tok_wlog)
                if logit_psfree:
                    te.wait_ge(sV.h, logit_psfree)
                instr = None
                for c in range(NCH):
                    instr = te.matmul(ps_ctx[:, 0:24],
                                      lnT[:, c, t * P:(t + 1) * P],
                                      wlog_sb[:, c, :],
                                      start=(c == 0), stop=(c == NCH - 1))
                sPE.inc(instr)
            tok_logit = sPE.n

            @block.vector
            def _(v):
                v.wait_ge(sPE.h, tok_logit)
                v.wait_ge(sV.h, tok_bases)
                sV.inc(v.tensor_scalar(out=Z[:, t], in0=ps_ctx[:, 0:16],
                                       scalar1=rms[:, t:t + 1], scalar2=10.0,
                                       op0=ALU.mult, op1=ALU.mult))
                sV.inc(v.tensor_scalar(out=PL[:, t, 0:4], in0=ps_ctx[:, 16:20],
                                       scalar1=rms[:, t:t + 1], scalar2=None, op0=ALU.mult))
                sV.inc(v.tensor_scalar(out=PL[:, t, 4:8], in0=ps_ctx[:, 20:24],
                                       scalar1=rms[:, t:t + 1], scalar2=None, op0=ALU.mult))
                v.wait_ge(sV.h, sV.n)
                sV.inc(v.tensor_tensor(out=Z[:, t], in0=Z[:, t], in1=bases[:, 0:16], op=ALU.add))
                sV.inc(v.tensor_tensor(out=PL[:, t, 0:4], in0=PL[:, t, 0:4],
                                       in1=bases[:, 16:20], op=ALU.add))
                sV.inc(v.tensor_tensor(out=PL[:, t, 4:8], in0=PL[:, t, 4:8],
                                       in1=bases[:, 20:24], op=ALU.add))
            logit_psfree = sV.n

            @block.vector
            def _(v):
                v.wait_ge(sV.h, logit_psfree)
                sV.inc(v.reduce_max(hps[:, t, 0:1], PL[:, t, 0:4], axis=AX.X))
                v.wait_ge(sV.h, sV.n)
                sV.inc(v.tensor_scalar(out=rsb[:, t], in0=PL[:, t, 0:4],
                                       scalar1=hps[:, t, 0:1], scalar2=None, op0=ALU.subtract))
            tok_hp1 = sV.n

            @block.scalar
            def _(s):
                s.wait_ge(sV.h, tok_hp1)
                sSC.inc(s.activation(out=lsb[:, t], in_=rsb[:, t], func=AF.Exp))
            tok_hp2 = sSC.n

            @block.vector
            def _(v):
                v.wait_ge(sSC.h, tok_hp2)
                sV.inc(v.reduce_sum(hps[:, t, 1:2], lsb[:, t], axis=AX.X))
                v.wait_ge(sV.h, sV.n)
                sV.inc(v.reciprocal(hps[:, t, 2:3], hps[:, t, 1:2]))
                v.wait_ge(sV.h, sV.n)
                sV.inc(v.tensor_scalar(out=Hpre[:, t], in0=lsb[:, t],
                                       scalar1=hps[:, t, 2:3], scalar2=None, op0=ALU.mult))
            tok_hpre = sV.n

            @block.vector
            def _(v):
                v.wait_ge(sV.h, tok_hpre)
                v.wait_ge(sV.h, xt_sc_tok)
                fs2, ft2 = xpb_free
                if ft2:
                    v.wait_ge(fs2.h, ft2)
                lnv = ln2v[t % 2]
                sV.inc(v.tensor_scalar(out=ostage, in0=lnv[:, 0:1024],
                                       scalar1=Hpre[:, t, 0:1], scalar2=None, op0=ALU.mult))
                for j in range(1, 4):
                    v.wait_ge(sV.h, sV.n)
                    sV.inc(v.scalar_tensor_tensor(out=ostage, in0=lnv[:, j * 1024:(j + 1) * 1024],
                                                  scalar=Hpre[:, t, j:j + 1], in1=ostage,
                                                  op0=ALU.mult, op1=ALU.add))
                v.wait_ge(sV.h, sV.n)
                sV.inc(v.tensor_copy(xpb, ostage))
            tok_xpb = sV.n

            bank = t % 2
            @block.tensor
            def _(te):
                te.wait_ge(sV.h, tok_xpb)
                te.wait_ge(sV.h, ps_tr_free[bank])
                instr = None
                for cj in range(8):
                    instr = te.transpose(
                        ps_tr[:, bank * 1024 + cj * P: bank * 1024 + (cj + 1) * P],
                        xpb[:, cj * P:(cj + 1) * P], ident[:])
                sPE.inc(instr)
            trtok = sPE.n
            xpb_free = (sPE, trtok)
            ln_free2[t % 2] = max(lnt_pe_tok, trtok)

            @block.vector
            def _(v):
                v.wait_ge(sPE.h, trtok)
                sV.inc(v.tensor_copy(
                    x_preT[:, :, t * P:(t + 1) * P],
                    ps_tr[:, bank * 1024:(bank + 1) * 1024].rearrange("p (j f) -> p j f", j=8)))
            ps_tr_free[bank] = sV.n
        xpre_done = sV.n
        z_done = sV.n

        ybr_done = 0
        for co in range(8):
            slot = co % 2
            wtok = emit_panel_load(wbr, co, 8, slot)
            mmtok, ps = emit_proj_mm(slot, 8, wtok, x_preT, extra_waits=[(sV, xpre_done)])

            @block.vector
            def _(v):
                v.wait_ge(sPE.h, mmtok)
                efs, eft = evac_free[slot]
                if eft:
                    v.wait_ge(efs.h, eft)
                sV.inc(v.tensor_copy(evac[:, slot], ps[:, :]))
            ps_free[slot] = (sV, sV.n)
            etok = sV.n

            bank = co % 2
            @block.tensor
            def _(te):
                te.wait_ge(sV.h, etok)
                te.wait_ge(sV.h, ps_tr_free[bank])
                instr = None
                for tt in range(TT):
                    instr = te.transpose(
                        ps_tr[:, bank * 1024 + tt * P: bank * 1024 + (tt + 1) * P],
                        evac[:, slot, tt * P:(tt + 1) * P], ident[:])
                sPE.inc(instr)
            evac_free[slot] = (sPE, sPE.n)
            trtok = sPE.n

            @block.vector
            def _(v):
                v.wait_ge(sPE.h, trtok)
                if co == 0:
                    v.wait_ge(sPE.h, max(kh_busy_slot))
                sV.inc(v.tensor_copy(
                    y_all[:, :, co * P:(co + 1) * P],
                    ps_tr[:, bank * 1024:(bank + 1) * 1024].rearrange("p (j f) -> p j f", j=TT)))
            ps_tr_free[bank] = sV.n
        ybr_done = sV.n

        sc_base = sSC.n
        v_base = sV.n
        NIT2 = 2 * N_ITER
        vt = lambda k, i: v_base + 10 * k + i
        st = lambda k, i: sc_base + 4 * k + i

        def gviews(g):
            sl = slice(4 * g, 4 * (g + 1))
            return dict(
                Zf=Z[:, sl, :].rearrange("p t e -> p (t e)"),
                Zsf=Zs[:, sl, :].rearrange("p t e -> p (t e)"),
                Ef=Ebuf[:, sl, :].rearrange("p t e -> p (t e)"),
                Z4r=Z[:, sl, :].rearrange("p t (i j) -> p t i j", j=4),
                Z4c=Z[:, sl, :].rearrange("p t (i j) -> p t j i", j=4),
                Zs4r=Zs[:, sl, :].rearrange("p t (i j) -> p t i j", j=4),
                Zs4c=Zs[:, sl, :].rearrange("p t (i j) -> p t j i", j=4),
                E4r=Ebuf[:, sl, :].rearrange("p t (i j) -> p t i j", j=4),
                E4c=Ebuf[:, sl, :].rearrange("p t (i j) -> p t j i", j=4),
                mx=mxb[:, sl, :], rs=rsb[:, sl, :], ls=lsb[:, sl, :],
            )
        GV = [gviews(0), gviews(1)]

        @block.scalar
        def _(s):
            for k in range(NIT2):
                for g in range(2):
                    s.wait_ge(sV.h, vt(k, 2 + 2 * g))
                    sSC.inc(s.activation(out=GV[g]["Ef"], in_=GV[g]["Zsf"], func=AF.Exp))
                    assert sSC.n == st(k, 1 + g)
                for g in range(2):
                    s.wait_ge(sV.h, vt(k, 5 + g))
                    sSC.inc(s.activation(out=GV[g]["ls"], in_=GV[g]["rs"], func=AF.Ln))
                    assert sSC.n == st(k, 3 + g)
            s.wait_ge(sV.h, vt(NIT2 - 1, 8))
            sSC.inc(s.activation(out=GV[0]["Ef"], in_=GV[0]["Zf"], func=AF.Exp))
            s.wait_ge(sV.h, vt(NIT2 - 1, 10))
            sSC.inc(s.activation(out=GV[1]["Ef"], in_=GV[1]["Zf"], func=AF.Exp))
            sSC.inc(s.activation(out=gtmp[:, 0:1], in_=gate_sb[:, 0:1], func=AF.Exp, scale=-1.0))
        assert sSC.n == sc_base + 4 * NIT2 + 3
        tok_hres = sc_base + 4 * NIT2 + 2
        tok_eg = sSC.n

        @block.vector
        def _(v):
            v.wait_ge(sV.h, z_done)
            for k in range(NIT2):
                rows = (k % 2 == 0)
                for g in range(2):
                    Zv = GV[g]["Z4r"] if rows else GV[g]["Z4c"]
                    Zsv = GV[g]["Zs4r"] if rows else GV[g]["Zs4c"]
                    if k > 0:
                        v.wait_ge(sV.h, vt(k - 1, 8 + 2 * g))
                    sV.inc(v.reduce_max(GV[g]["mx"], Zv, axis=AX.X))
                    v.wait_ge(sV.h, sV.n)
                    mB = GV[g]["mx"].unsqueeze(3).broadcast_to([P, 4, 4, 4])
                    sV.inc(v.tensor_tensor(out=Zsv, in0=Zv, in1=mB, op=ALU.subtract))
                    assert sV.n == vt(k, 2 + 2 * g)
                for g in range(2):
                    Ev = GV[g]["E4r"] if rows else GV[g]["E4c"]
                    v.wait_ge(sSC.h, st(k, 1 + g))
                    sV.inc(v.reduce_sum(GV[g]["rs"], Ev, axis=AX.X))
                    assert sV.n == vt(k, 5 + g)
                for g in range(2):
                    Zv = GV[g]["Z4r"] if rows else GV[g]["Z4c"]
                    v.wait_ge(sSC.h, st(k, 3 + g))
                    sV.inc(v.tensor_tensor(out=GV[g]["ls"], in0=GV[g]["ls"],
                                           in1=GV[g]["mx"], op=ALU.add))
                    v.wait_ge(sV.h, sV.n)
                    lsB = GV[g]["ls"].unsqueeze(3).broadcast_to([P, 4, 4, 4])
                    sV.inc(v.tensor_tensor(out=Zv, in0=Zv, in1=lsB, op=ALU.subtract))
                    assert sV.n == vt(k, 8 + 2 * g)
        assert sV.n == v_base + 10 * NIT2

        @block.vector
        def _(v):
            v.wait_ge(sSC.h, tok_eg)
            sV.inc(v.tensor_scalar(out=gtmp[:, 0:1], in0=gtmp[:, 0:1], scalar1=1.0,
                                   scalar2=None, op0=ALU.add))
            v.wait_ge(sV.h, sV.n)
            sV.inc(v.reciprocal(gtmp[:, 1:2], gtmp[:, 0:1]))
            v.wait_ge(sV.h, sV.n)
            sV.inc(v.tensor_scalar(out=gtmp[:, 0:1], in0=gtmp[:, 1:2], scalar1=-1.0,
                                   scalar2=1.0, op0=ALU.mult, op1=ALU.add))
        tok_g = sV.n

        @block.vector
        def _(v):
            v.wait_ge(sV.h, logit_psfree)
            sV.inc(v.reduce_max(hsc[:, :, 0:1], PL[:, :, 4:8], axis=AX.X))
            v.wait_ge(sV.h, sV.n)
            sV.inc(v.tensor_tensor(out=Hpost[:, :, :], in0=PL[:, :, 4:8],
                                   in1=hsc[:, :, 0:1].broadcast_to([P, TT, 4]), op=ALU.subtract))
        tok_po1 = sV.n

        @block.scalar
        def _(s):
            s.wait_ge(sV.h, tok_po1)
            sSC.inc(s.activation(out=Hpost[:, :, :], in_=Hpost[:, :, :], func=AF.Exp))
        tok_po2 = sSC.n

        @block.vector
        def _(v):
            v.wait_ge(sSC.h, tok_po2)
            sV.inc(v.reduce_sum(hsc[:, :, 1:2], Hpost[:, :, :], axis=AX.X))
            v.wait_ge(sV.h, sV.n)
            sV.inc(v.reciprocal(hsc[:, :, 0:1], hsc[:, :, 1:2]))
            v.wait_ge(sV.h, sV.n)
            sV.inc(v.tensor_tensor(out=Hpost[:, :, :], in0=Hpost[:, :, :],
                                   in1=hsc[:, :, 0:1].broadcast_to([P, TT, 4]), op=ALU.mult))
        tok_hpost = sV.n

        @block.vector
        def _(v):
            v.wait_ge(sSC.h, tok_hres)
            v.wait_ge(sV.h, tok_g)
            for t in range(TT):
                sV.inc(v.tensor_tensor(out=gtmp[:, 2:3], in0=gtmp[:, 1:2],
                                       in1=rstd[:, t:t + 1], op=ALU.mult))
                sV.inc(v.tensor_tensor(out=gtmp[:, 3:4], in0=gtmp[:, 0:1],
                                       in1=rstd[:, t:t + 1], op=ALU.mult))
                v.wait_ge(sV.h, sV.n)
                sV.inc(v.tensor_scalar(out=coef[:, t], in0=Ebuf[:, t],
                                       scalar1=gtmp[:, 2:3], scalar2=None, op0=ALU.mult))
                v.wait_ge(sV.h, sV.n)
                for i in range(4):
                    sV.inc(v.tensor_scalar(out=coef[:, t, 5 * i:5 * i + 1],
                                           in0=coef[:, t, 5 * i:5 * i + 1],
                                           scalar1=gtmp[:, 3:4], scalar2=None, op0=ALU.add))
                v.wait_ge(sV.h, sV.n)
                sV.inc(v.reduce_sum(nb[:, t], coef[:, t].rearrange("p (i j) -> p i j", j=4),
                                    axis=AX.X))
                v.wait_ge(sV.h, sV.n)
                sV.inc(v.tensor_scalar(out=nb[:, t], in0=nb[:, t],
                                       scalar1=mv[:, t, 0:1], scalar2=-1.0,
                                       op0=ALU.mult, op1=ALU.mult))
        tok_coef = sV.n

        if "ffn" in taps:
            @block.vector
            def _(v):
                v.wait_ge(sV.h, tok_coef)
                v.wait_ge(sV.h, tok_hpost)
                sV.inc(v.tensor_copy(ostage[:, 0:16], coef[:, 0]))
                sV.inc(v.tensor_copy(ostage[:, 16:20], Hpre[:, 0]))
                sV.inc(v.tensor_copy(ostage[:, 20:24], Hpost[:, 0]))
                sV.inc(v.tensor_copy(ostage[:, 24:28], nb[:, 0]))
                sV.inc(v.tensor_copy(ostage[:, 28:44], Z[:, 0]))
                sV.inc(v.tensor_copy(ostage[:, 44:60], Ebuf[:, 0]))
            vt = sV.n

            @block.sync
            def _(sy):
                sy.wait_ge(sV.h, vt)
                sy.dma_start(out=tap_d["tap_ffn"][:, 0:96], in_=ostage[:, 0:96]).then_inc(sX.h, 16)
                sX.n += 16
            dump_dram_bf16
            for tt_ in range(TT):
                @block.vector
                def _(v):
                    v.wait_ge(sV.h, ybr_done)
                    v.wait_ge(sX.h, sX.n)
                    sV.inc(v.tensor_copy(ostage, y_all[:, tt_, :]))
                vt2 = sV.n

                @block.sync
                def _(sy):
                    sy.wait_ge(sV.h, vt2)
                    sy.dma_start(out=tap_d["tap_y"][tt_ * P:(tt_ + 1) * P, :], in_=ostage).then_inc(sX.h, 16)
                    sX.n += 16

        mix_xt_free = (sV, xt_free2[0])
        for t in range(TT):
            @block.sync
            def _(sy):
                fs, ft = mix_xt_free
                if ft:
                    sy.wait_ge(fs.h, ft)
                sy.dma_start(out=xt[:], in_=x2_d[t * P:(t + 1) * P, :]).then_inc(sXA.h, 16)
                sXA.n += 16
            tok_load = sXA.n

            last_mix_tok = 0
            for i in range(4):
                osl = (t * 4 + i) % 2
                ost = ostage2[osl]
                prev_store = sOUT.n - 16

                @block.vector
                def _(v):
                    v.wait_ge(sXA.h, tok_load)
                    v.wait_ge(sV.h, tok_coef)
                    v.wait_ge(sV.h, tok_hpost)
                    v.wait_ge(sV.h, ybr_done)
                    if prev_store > 0:
                        v.wait_ge(sOUT.h, prev_store)
                    sV.inc(v.scalar_tensor_tensor(out=ost, in0=xt[:, 0:1024],
                                                  scalar=coef[:, t, 4 * i:4 * i + 1],
                                                  in1=xt[:, i * 1024:(i + 1) * 1024],
                                                  op0=ALU.mult, op1=ALU.add))
                    for j in range(1, 4):
                        v.wait_ge(sV.h, sV.n)
                        sV.inc(v.scalar_tensor_tensor(out=ost,
                                                      in0=xt[:, j * 1024:(j + 1) * 1024],
                                                      scalar=coef[:, t, 4 * i + j:4 * i + j + 1],
                                                      in1=ost,
                                                      op0=ALU.mult, op1=ALU.add))
                    v.wait_ge(sV.h, sV.n)
                    sV.inc(v.scalar_tensor_tensor(out=ost, in0=y_all[:, t, :],
                                                  scalar=Hpost[:, t, i:i + 1], in1=ost,
                                                  op0=ALU.mult, op1=ALU.add))
                    v.wait_ge(sV.h, sV.n)
                    sV.inc(v.tensor_scalar(out=ost, in0=ost,
                                           scalar1=nb[:, t, i:i + 1], scalar2=None, op0=ALU.add))
                mtok = sV.n
                last_mix_tok = mtok

                @block.sync
                def _(sy):
                    sy.wait_ge(sV.h, mtok)
                    sy.dma_start(out=out_d[t * P:(t + 1) * P, i * 1024:(i + 1) * 1024],
                                 in_=ost).then_inc(sOUT.h, 16)
                    sOUT.n += 16
            mix_xt_free = (sV, last_mix_tok)

        @block.sync
        def _(sy):
            sy.wait_ge(sOUT.h, sOUT.n)

    return nc


_PROG_CACHE = {}


def _get_prog(taps=()):
    key = tuple(sorted(taps))
    if key not in _PROG_CACHE:
        _PROG_CACHE[key] = build_program(key)
    return _PROG_CACHE[key]


def make_in_maps(inputs):
    x = np.asarray(inputs["L0_attn_hidden_states"], dtype=np.float32)
    wq = np.asarray(inputs["L0_attn_Wq"], dtype=np.float32)
    wk = np.asarray(inputs["L0_attn_Wk"], dtype=np.float32)
    wv = np.asarray(inputs["L0_attn_Wv"], dtype=np.float32)
    bv = np.asarray(inputs["L0_attn_bv"], dtype=np.float32)
    wo = np.asarray(inputs["L0_attn_Wo"], dtype=np.float32)
    w_res = np.asarray(inputs["L0_ffn_W_res"], dtype=np.float32)
    w_pre = np.asarray(inputs["L0_ffn_W_pre"], dtype=np.float32)
    w_post = np.asarray(inputs["L0_ffn_W_post"], dtype=np.float32)
    hres_b = np.asarray(inputs["L0_ffn_H_res_base"], dtype=np.float32)
    hpre_b = np.asarray(inputs["L0_ffn_H_pre_base"], dtype=np.float32)
    hpost_b = np.asarray(inputs["L0_ffn_H_post_base"], dtype=np.float32)
    gate = np.asarray(inputs["L0_ffn_gate"], dtype=np.float32)
    wbr_ = np.asarray(inputs["L0_ffn_W_branch"], dtype=np.float32)

    shared = {
        "wqT": np.ascontiguousarray(wq.T),
        "wkT": np.ascontiguousarray(wk.T),
        "wvT": np.ascontiguousarray(wv.T),
        "woT": np.ascontiguousarray(wo.T),
        "wbr": np.ascontiguousarray(wbr_),
        "wlogT": np.ascontiguousarray(np.concatenate([w_res, w_pre, w_post], axis=0).T),
        "bases_in": np.ascontiguousarray(np.broadcast_to(
            np.concatenate([hres_b.reshape(-1), hpre_b, hpost_b])[None, :], (P, 24))),
        "gate_in": np.ascontiguousarray(np.broadcast_to(gate.reshape(1, 1), (P, 1))),
        "bv_in": np.ascontiguousarray(bv.reshape(NKV, P).T),
    }
    in_maps = []
    for c in range(8):
        b, chunk = c // 4, c % 4
        m = dict(shared)
        m["x_in"] = np.ascontiguousarray(x[b, chunk * TOK:(chunk + 1) * TOK, :])
        in_maps.append(m)
    return in_maps


def kernel(**inputs):
    nc = _get_prog()
    in_maps = make_in_maps(inputs)
    res = run_bass_kernel_spmd(nc, in_maps, core_ids=list(range(8)))
    out = np.empty((2, S_FULL, D), dtype=np.float32)
    for c in range(8):
        b, chunk = c // 4, c % 4
        out[b, chunk * TOK:(chunk + 1) * TOK, :] = np.asarray(res.results[c]["out"])
    return out
